# version 32
# baseline (speedup 1.0000x reference)
import contextlib
import numpy as np
import ml_dtypes
import concourse.bass as bass
import concourse.mybir as mybir
from concourse.bass_utils import run_bass_kernel_spmd

F32 = mybir.dt.float32
BF16 = mybir.dt.bfloat16
U8 = mybir.dt.uint8
AF = mybir.ActivationFunctionType
ALU = mybir.AluOpType

S, D, P, T, H, DH, FF = 4096, 1024, 128, 32, 16, 64, 4096
NCH = D // P
NFCH = FF // P
NSUP = T // 2
INV_D = 1.0 / D
EPS_LN = 1e-5
EPS_Z = 1e-6
BF = np.dtype(ml_dtypes.bfloat16)


class Ctr:
    def __init__(self):
        self.c = {}
        self.marks = {}

    def inc(self, name, n=1, key=None):
        self.c[name] = self.c.get(name, 0) + n
        if key is not None:
            self.marks[key] = (name, self.c[name])
        return self.c[name]


def build_nc(debug=False, phases=5, marks=None):
    nc = bass.Bass("TRN2", target_bir_lowering=False)

    x_d = nc.dram_tensor("x", [S, D], F32, kind="ExternalInput").ap()
    wq_d = nc.dram_tensor("Wq", [D, D], F32, kind="ExternalInput").ap()
    wk_d = nc.dram_tensor("Wk", [D, D], F32, kind="ExternalInput").ap()
    wv_d = nc.dram_tensor("Wv", [D, D], F32, kind="ExternalInput").ap()
    wo_d = nc.dram_tensor("Wo", [D, D], F32, kind="ExternalInput").ap()
    w1_d = nc.dram_tensor("W1", [D, FF], F32, kind="ExternalInput").ap()
    w2_d = nc.dram_tensor("W2", [FF, D], F32, kind="ExternalInput").ap()
    wf_d = nc.dram_tensor("Wf", [D, D], F32, kind="ExternalInput").ap()
    dsh_d = nc.dram_tensor("disha", [DH, D], F32, kind="ExternalInput").ap()
    bq_d = nc.dram_tensor("bq", [1, D], F32, kind="ExternalInput").ap()
    bk_d = nc.dram_tensor("bk", [1, D], F32, kind="ExternalInput").ap()
    bv_d = nc.dram_tensor("bv", [1, D], F32, kind="ExternalInput").ap()
    bo_d = nc.dram_tensor("bo", [1, D], F32, kind="ExternalInput").ap()
    b1_d = nc.dram_tensor("b1c", [P, NFCH], F32, kind="ExternalInput").ap()
    b2_d = nc.dram_tensor("b2", [1, D], F32, kind="ExternalInput").ap()
    g1_d = nc.dram_tensor("g1r", [P, D], F32, kind="ExternalInput").ap()
    be1_d = nc.dram_tensor("be1r", [P, D], F32, kind="ExternalInput").ap()
    g2_d = nc.dram_tensor("g2r", [P, D], F32, kind="ExternalInput").ap()
    be2_d = nc.dram_tensor("be2r", [P, D], F32, kind="ExternalInput").ap()
    idb_d = nc.dram_tensor("identb", [P, P], BF16, kind="ExternalInput").ap()
    idf_d = nc.dram_tensor("identf", [P, P], F32, kind="ExternalInput").ap()
    iplo_d = nc.dram_tensor("ipadlo", [DH, P], BF16, kind="ExternalInput").ap()
    iphi_d = nc.dram_tensor("ipadhi", [DH, P], BF16, kind="ExternalInput").ap()
    istk_d = nc.dram_tensor("istack", [P, DH], BF16, kind="ExternalInput").ap()
    ones_d = nc.dram_tensor("onesr", [1, 512], BF16, kind="ExternalInput").ap()
    onec_d = nc.dram_tensor("onesc", [P, 1], BF16, kind="ExternalInput").ap()
    epsz_d = nc.dram_tensor("epsz", [1, H], BF16, kind="ExternalInput").ap()

    out_d = nc.dram_tensor("out", [S, D], F32, kind="ExternalOutput").ap()
    skind = "ExternalOutput" if debug else "Internal"
    qt_d = nc.dram_tensor("qt_s", [T, P, D], BF16, kind=skind).ap()
    x1_d = nc.dram_tensor("x1_s", [T, P, D], F32, kind=skind).ap()
    u_d = nc.dram_tensor("u_s", [T, P, D], F32, kind=skind).ap()

    ARENA = 212736
    es = contextlib.ExitStack()
    with es:
        arena = es.enter_context(nc.sbuf_tensor("arena", [P, ARENA], U8))
        av = arena.ap()

        def VF(off, n, part=P, p0=0):
            return av[p0:p0 + part, off:off + 4 * n].bitcast(F32)

        def VB(off, n, part=P, p0=0):
            return av[p0:p0 + part, off:off + 2 * n].bitcast(BF16)

        xbuf = [VF(0, D), VF(4096, D)]
        qtsup = VB(8192, NCH * 256).rearrange("p (c s) -> p c s", c=NCH)
        qts4 = VB(8192, NCH * 256).rearrange("p (u c s) -> p u c s", u=2, c=NCH)
        stf32 = [VF(8192, D), VF(12288, D)]
        srcTs = VB(16384, NCH * 256).rearrange("p (c s) -> p c s", c=NCH)
        qtbuf = [VB(16384, D).rearrange("p (c s) -> p c s", c=NCH),
                 VB(18432, D).rearrange("p (c s) -> p c s", c=NCH)]
        tf32 = VF(20480, D)
        ldf32 = [VF(16384, D), VF(20480, D)]
        ldf3 = [VF(16384, D), VF(20480, D), VF(4096, D)]
        scr1 = VF(24576, D)
        scr2bf = VB(28672, D)
        x1bf2 = VB(30720, D)
        x1bfB = VB(209920, D)
        scr2 = VF(28672, D)
        kvsb = VB(24576, D, part=DH)
        x1Ts = VB(24576, NCH * 256).rearrange("p (c s) -> p c s", c=NCH)
        x1Ts2 = VB(0, NCH * 256).rearrange("p (c s) -> p c s", c=NCH)
        wfstage = VB(28672, D)
        scrA = VF(24576, D)
        scrA2 = VF(78848, D)
        xAv2 = VF(82944, D)
        nbf2 = VB(87040, D)
        nT2 = VB(89088, D).rearrange("p (c s) -> p c s", c=NCH)
        poolT2 = VB(91136, P, part=DH)
        xAv3 = VF(93184, D)
        xAv4 = VF(97280, D)
        xb3 = VF(128000, D)
        zdum = VB(132096, D)
        xAv = VF(28672, D)
        srcbf = VB(32768, D)
        srcbf2 = VB(12288, D)
        kbf = VB(34816, D)
        vbf = VB(36864, D)
        obf = VB(32768, D)
        oTv = VB(34816, D)
        kvz = VB(36864, D)
        x1bf = VB(32768, D)
        hT = [VB(34816, 256), VB(35328, 256)]
        nbf = VB(32768, D)
        nTv = VB(34816, D).rearrange("p (c s) -> p c s", c=NCH)
        dshb = VB(36864, D, part=DH)
        o6 = 38912
        identb = VB(o6, P)
        identf = VF(o6 + 256, P)
        iplo = VB(o6 + 768, P, part=DH)
        iphi = VB(o6 + 1024, P, part=DH)
        istk = VB(o6 + 1280, DH)
        onec = VB(o6 + 1408, 1)
        m0c = VF(o6 + 1472, T)
        r0c = VF(o6 + 1600, T)
        usc = VF(o6 + 1728, T)
        uqc = VF(o6 + 1856, T)
        ksz = VB(o6 + 1984, NCH * H).rearrange("p (c h) -> p c h", c=NCH)
        zscrT = VF(o6 + 2240, P, part=H)
        zcols = VF(o6 + 2752, H)
        zcols2 = VF(o6 + 2928, H)
        cc = VF(o6 + 2816, 16)
        ec = VF(o6 + 2880, 1)
        km1 = VF(o6 + 2884, 1)
        kmh = VF(o6 + 2888, 1)
        ksb = VB(o6 + 2892, H, part=DH)
        poolT = VB(o6 + 3000, P, part=DH)
        epszr = VB(o6 + 3456, H, part=1)
        nc.const_aps.register(F32, EPS_LN, ec[:, 0:1])
        nc.const_aps.register(F32, -1.0, km1[:, 0:1])
        nc.const_aps.register(F32, -0.5, kmh[:, 0:1])
        b1c = VF(o6 + 3300, NFCH)
        o7 = 43008
        bqr = VB(o7, D, part=1, p0=0)
        bkr = VB(o7 + 2048, D, part=1, p0=0)
        bvr = VB(o7 + 4096, D, part=1, p0=0)
        bor = VB(o7 + 6144, D, part=1, p0=0)
        b2r = VB(o7 + 8192, D, part=1, p0=0)
        onesr = VB(o7 + 10240, 512, part=1, p0=0)
        wo_v = VB(54272, NCH * D).rearrange("p (c f) -> p c f", c=NCH)
        wfT = VB(54272, NCH * D).rearrange("p (c f) -> p c f", c=NCH)
        g1r = VF(70656, D)
        be1r = VF(74752, D)
        g2r = VF(70656, D)
        be2r = VF(74752, D)
        wq_v = VB(78848, NCH * D).rearrange("p (c f) -> p c f", c=NCH)
        wk_v = VB(95232, NCH * D).rearrange("p (c f) -> p c f", c=NCH)
        wv_v = VB(111616, NCH * D).rearrange("p (c f) -> p c f", c=NCH)
        w1_v = [VB(78848 + 8192 * c, FF) for c in range(NCH)]
        w2_v = [VB(144384 + 2048 * c, D) for c in range(NFCH)]

        ps_bf = es.enter_context(nc.psum_tensor("ps_bf", [P, 1024], BF16))
        psA = es.enter_context(nc.psum_tensor("psA", [P, 1024], F32))
        psB = es.enter_context(nc.psum_tensor("psB", [P, 1024], F32))
        psC = es.enter_context(nc.psum_tensor("psC", [P, 1024], F32))
        psD = es.enter_context(nc.psum_tensor("psD", [P, 512], F32))

        sem = {}
        for name in ["qx0", "qx1", "qw", "qg_rows", "qg_wk", "qg_wv",
                     "qg_wq", "qg_wo", "qg_w2", "qg_w1", "qg_wf", "qg_dsh", "qts",
                     "qtl0", "qtl1", "qs0", "qs1", "ql0", "ql1", "ql2",
                     "qg_xc0", "qg_xc1", "qx2",
                     "pe", "dv", "ac", "po"]:
            sem[name] = es.enter_context(nc.semaphore(name))
        ct = Ctr()

        lookup = marks if marks is not None else ct.marks

        def W(eng, key):
            if key in lookup:
                name, val = lookup[key]
                eng.wait_ge(sem[name], val)

        def I(inst, name, n=1, key=None):
            inst.then_inc(sem[name], n)
            ct.inc(name, n, key=key)

        def serial(eng, name):
            class _P:
                def __getattr__(self, attr):
                    if attr in ("wait_ge",):
                        return getattr(eng, attr)
                    fn = getattr(eng, attr)

                    def call(*a, key=None, **kw):
                        c = ct.c.get(name, 0)
                        if c > 0:
                            eng.wait_ge(sem[name], c)
                        inst = fn(*a, **kw)
                        inst.then_inc(sem[name], 1)
                        ct.inc(name, 1, key=key)
                        return inst
                    return call
            return _P()

        DMA_SEMS = ["qx0", "qx1", "qw", "qg_rows", "qg_wk", "qg_wv", "qg_wq",
                    "qg_wo", "qg_w2", "qg_w1", "qg_wf", "qg_dsh", "qts",
                    "qtl0", "qtl1", "qs0", "qs1", "ql0", "ql1", "ql2",
                    "qg_xc0", "qg_xc1", "qx2"]

        def drain_tail():
            with nc.Block("drain") as dblk:
                @dblk.sync
                def _(sy):
                    for name in DMA_SEMS:
                        if ct.c.get(name, 0) > 0:
                            sy.wait_ge(sem[name], ct.c[name])

        with nc.Block("init", no_gpsimd_drain=True) as blk:
            @blk.sync
            def _(sy):
                for dst, src in [(identb, idb_d), (identf, idf_d),
                                 (iplo, iplo_d), (iphi, iphi_d), (istk, istk_d),
                                 (onec, onec_d), (onesr, ones_d), (b1c, b1_d),
                                 (epszr, epsz_d),
                                 (g1r, g1_d), (be1r, be1_d)]:
                    sy.dma_start(dst[:, :], src).then_inc(sem["qw"], 16)
                    ct.inc("qw", 16)
                ct.marks["w_consts"] = ("qw", ct.c["qw"])

            @blk.gpsimd
            def _(gp):
                for dst, src in [(bqr, bq_d), (bkr, bk_d), (bvr, bv_d),
                                 (bor, bo_d), (b2r, b2_d)]:
                    gp.dma_start(dst[:, :], src).then_inc(sem["qg_rows"], 16)
                    ct.inc("qg_rows", 16)
                ct.marks["w_rows"] = ("qg_rows", ct.c["qg_rows"])
                for w in range(NCH):
                    gp.dma_start(wk_v[:, w, :], wk_d[w * P:(w + 1) * P, :]
                                 ).then_inc(sem["qg_wk"], 16)
                    ct.inc("qg_wk", 16, key=f"w_wk{w}")
                for w in range(NCH):
                    gp.dma_start(wv_v[:, w, :], wv_d[w * P:(w + 1) * P, :]
                                 ).then_inc(sem["qg_wv"], 16)
                    ct.inc("qg_wv", 16, key=f"w_wv{w}")
                for w in range(NCH):
                    gp.dma_start(wq_v[:, w, :], wq_d[w * P:(w + 1) * P, :]
                                 ).then_inc(sem["qg_wq"], 16)
                    ct.inc("qg_wq", 16, key=f"w_wq{w}")

            @blk.vector
            def _(v):
                v = serial(v, "dv")
                v.memset(ec[:, 0:1], EPS_LN)
                v.memset(km1[:, 0:1], -1.0)
                v.memset(kmh[:, 0:1], -0.5)
                v.memset(ksz[:, :, :], 0.0, key="zmem")

        if phases < 1:
            drain_tail()
            return nc, ct
        with nc.Block("passA", no_gpsimd_drain=True) as blk:
            xbufA = [xbuf[0], xbuf[1], xb3]

            def psK(t):
                return psA if t % 2 == 0 else psB

            def psV(t):
                return psB if t % 2 == 0 else psA

            @blk.sync
            def _(sy):
                for t in range(T):
                    W(sy, f"aV3_{t - 3}")
                    qn = ["qx0", "qx1", "qx2"][t % 3]
                    sy.dma_start(xbufA[t % 3][:, :],
                                 x_d[t * P:(t + 1) * P, :]
                                 ).then_inc(sem[qn], 16)
                    ct.inc(qn, 16, key=f"ax_{t}")

            @blk.gpsimd
            def _(gp):
                for st in range(NSUP):
                    if st < NCH:
                        gp.dma_start(wo_v[:, st, :], wo_d[st * P:(st + 1) * P, :]
                                     ).then_inc(sem["qg_wo"], 16)
                        ct.inc("qg_wo", 16, key=f"w_wo{st}")
                    for c in (2 * st, 2 * st + 1):
                        gp.dma_start(w2_v[c][:, :], w2_d[c * P:(c + 1) * P, :]
                                     ).then_inc(sem["qg_w2"], 16)
                        ct.inc("qg_w2", 16, key=f"w_w2{c}")

            @blk.scalar
            def _(sc):
                raw = sc
                sc = serial(sc, "ac")

                def zst(t):
                    if t >= T:
                        return
                    W(sc, f"ax_{t}")
                    W(sc, f"zV2_{t - 2}")
                    p_ = t % 2
                    sc.activation(tf32[:, :], xbufA[t % 3][:, :], AF.Copy,
                                  accum_out=cc[:, p_:p_ + 1],
                                  key=f"zA1_{t}")

                def za2(t):
                    if t >= T:
                        return
                    p_ = t % 2
                    W(sc, f"zV2_{t}")
                    sc.activation(r0c[:, t:t + 1], cc[:, 2 + p_:3 + p_],
                                  AF.Ln, bias=EPS_LN)
                    sc.activation(r0c[:, t:t + 1], r0c[:, t:t + 1], AF.Exp,
                                  scale=-0.5, key=f"zA2_{t}")

                zst(0)
                zst(1)
                za2(0)
                za2(1)
                for st in range(NSUP):
                    for sub in range(2):
                        t = 2 * st + sub
                        if t >= 1:
                            za2(t + 1)
                        zst(t + 2)
                        W(sc, f"aP2_{t}")
                        W(sc, f"aV5_{t - 1}")
                        sc.activation(scr2[:, :], psK(t)[:, :], AF.Relu,
                                      scale=-1.0)
                        sc.activation(scr2[:, :], scr2[:, :], AF.Exp,
                                      scale=-1.0, key=f"aA3_{t}")
                        W(sc, f"aP3_{t}")
                        W(sc, f"aP4_{t - 1}")
                        sc.activation(vbf[:, :], psV(t)[:, :], AF.Copy,
                                      key=f"aA4_{t}")
                    for g in range(4):
                        half = scr1[:, (g % 2) * 512:(g % 2) * 512 + 512]
                        pq = psA[:, (g % 2) * 512:(g % 2) * 512 + 512]
                        W(sc, f"aPQg_{st}_{g}")
                        if g >= 2:
                            W(sc, f"aVQg_{st}_{g - 2}")
                        else:
                            W(sc, f"aVQg_{st - 1}_{g + 2}")
                        sc.activation(half, pq, AF.Relu, scale=-1.0)
                        sc.activation(half, half, AF.Exp, scale=-1.0,
                                      key=f"aAQg_{st}_{g}")
                    W(raw, f"aVQg_{st}_3")
                    for sub in range(2):
                        raw.dma_start(
                            qt_d[2 * st + sub],
                            qts4[:, sub, :, :].rearrange("p c s -> p (c s)")
                        ).then_inc(sem["qts"], 16)
                        ct.inc("qts", 16, key=f"aqts_{st}_{sub}")

            @blk.vector
            def _(v):
                v = serial(v, "dv")
                srcbfP = [srcbf, srcbf2]

                def emit_v3(t):
                    if t >= T:
                        return
                    W(v, f"ax_{t}")
                    W(v, f"aP1_{t - 2}")
                    W(v, f"zA2_{t}")
                    v.tensor_scalar(srcbfP[t % 2][:, :], xbufA[t % 3][:, :],
                                    m0c[:, t:t + 1], r0c[:, t:t + 1],
                                    ALU.subtract, ALU.mult,
                                    key=f"aV3_{t}")

                def zsq(t):
                    if t >= T:
                        return
                    p_ = t % 2
                    W(v, f"ax_{t}")
                    v.scalar_tensor_tensor(zdum[:, :], xbufA[t % 3][:, :],
                                           1.0, xbufA[t % 3][:, :],
                                           ALU.mult, ALU.mult,
                                           accum_out=cc[:, 2 + p_:3 + p_],
                                           key=f"zSq_{t}")

                def zcols(t):
                    if t >= T:
                        return
                    p_ = t % 2
                    W(v, f"zA1_{t}")
                    v.tensor_scalar(m0c[:, t:t + 1], cc[:, p_:p_ + 1],
                                    INV_D, None, ALU.mult)
                    v.tensor_tensor(cc[:, p_:p_ + 1], m0c[:, t:t + 1],
                                    m0c[:, t:t + 1], ALU.mult)
                    v.tensor_scalar(cc[:, 2 + p_:3 + p_],
                                    cc[:, 2 + p_:3 + p_], INV_D,
                                    cc[:, p_:p_ + 1], ALU.mult,
                                    ALU.subtract, key=f"zV2_{t}")

                def emit_v4(t):
                    sub = t % 2
                    W(v, f"aP1_{t}")
                    v.tensor_copy(srcTs[:, 0:4, sub * P:(sub + 1) * P],
                                  ps_bf[:, 0:512].rearrange(
                                      "p (c s) -> p c s", c=4),
                                  key=f"aV4a_{t}")
                    v.tensor_copy(srcTs[:, 4:8, sub * P:(sub + 1) * P],
                                  ps_bf[:, 512:1024].rearrange(
                                      "p (c s) -> p c s", c=4),
                                  key=f"aV4_{t}")

                def emit_v5(t):
                    if t < 0:
                        return
                    W(v, f"aA3_{t}")
                    W(v, f"aP4_{t - 1}")
                    v.scalar_tensor_tensor(kbf[:, :], psK(t)[:, :], 0.0,
                                           scr2[:, :], ALU.max, ALU.add,
                                           key=f"aV5_{t}")

                zsq(0)
                zcols(0)
                zsq(1)
                zcols(1)
                emit_v3(0)
                emit_v3(1)
                for st in range(NSUP):
                    for sub in range(2):
                        t = 2 * st + sub
                        emit_v4(t)
                        emit_v5(t - 1)
                        if t >= 1:
                            emit_v3(t + 1)
                        zsq(t + 2)
                        zcols(t + 2)
                    for g in range(4):
                        half = scr1[:, (g % 2) * 512:(g % 2) * 512 + 512]
                        pq = psA[:, (g % 2) * 512:(g % 2) * 512 + 512]
                        W(v, f"aAQg_{st}_{g}")
                        if g == 0:
                            W(v, f"aqts_{st - 1}_1")
                        for u_ in range(2):
                            v.scalar_tensor_tensor(
                                qts4[:, u_, 2 * g:2 * g + 2, :],
                                pq.rearrange("p (c u s) -> p c u s",
                                             c=2, u=2)[:, :, u_, :],
                                0.0,
                                half.rearrange("p (c u s) -> p c u s",
                                               c=2, u=2)[:, :, u_, :],
                                ALU.max, ALU.add,
                                key=(f"aVQg_{st}_{g}" if u_ == 1 else None))
                emit_v5(T - 1)

            @blk.tensor
            def _(te):
                W(te, "w_consts")

                def emit_p4(t):
                    first = (t == 0)
                    last = (t == T - 1)
                    W(te, f"aV5_{t}")
                    W(te, f"aA4_{t}")
                    for h in range(H):
                        col = (h % 2) * 512 + (h // 2) * DH
                        te.matmul(psC[0:DH, col:col + DH],
                                  kbf[:, h * DH:(h + 1) * DH],
                                  vbf[:, h * DH:(h + 1) * DH],
                                  start=(first and h < 2),
                                  stop=(last and h >= H - 2))
                    for h in range(H):
                        colz = (h % 2) * 8 + (h // 2)
                        ii = te.matmul(psD[0:DH, colz:colz + 1],
                                       kbf[:, h * DH:(h + 1) * DH],
                                       onec[:, 0:1],
                                       start=(first and h == 0),
                                       stop=(last and h == H - 1))
                    I(ii, "pe", key=f"aP4_{t}")

                for st in range(NSUP):
                    for sub in range(2):
                        t = 2 * st + sub
                        W(te, f"aV3_{t}")
                        W(te, f"aV4_{t - 1}")
                        sbp = srcbf if t % 2 == 0 else srcbf2
                        for c in range(NCH):
                            ii = te.transpose(ps_bf[:, c * P:(c + 1) * P],
                                              sbp[:, c * P:(c + 1) * P],
                                              identb[:, :])
                        I(ii, "pe", key=f"aP1_{t}")
                        W(te, f"aV4a_{t}")
                        W(te, f"w_wk{NCH - 1}")
                        W(te, "w_rows")
                        if sub == 0:
                            W(te, f"aVQg_{st - 1}_3")
                        else:
                            W(te, f"aA4_{t - 1}")
                        for w in range(NCH):
                            if w == 4:
                                W(te, f"aV4_{t}")
                            for nh in range(2):
                                te.matmul(psK(t)[:, nh * 512:(nh + 1) * 512],
                                          srcTs[:, w, sub * P:(sub + 1) * P],
                                          wk_v[:, w, nh * 512:(nh + 1) * 512],
                                          start=(w == 0), stop=False)
                        for nh in range(2):
                            ii = te.matmul(psK(t)[:, nh * 512:(nh + 1) * 512],
                                           onesr[:, 0:P],
                                           bkr[:, nh * 512:(nh + 1) * 512],
                                           start=False, stop=True)
                        I(ii, "pe", key=f"aP2_{t}")
                        W(te, f"w_wv{NCH - 1}")
                        if sub == 0:
                            W(te, f"aVQg_{st - 1}_3")
                            W(te, f"aV5_{t - 1}")
                        else:
                            W(te, f"aV5_{t - 1}")
                        for w in range(NCH):
                            for nh in range(2):
                                te.matmul(psV(t)[:, nh * 512:(nh + 1) * 512],
                                          srcTs[:, w, sub * P:(sub + 1) * P],
                                          wv_v[:, w, nh * 512:(nh + 1) * 512],
                                          start=(w == 0), stop=False)
                        for nh in range(2):
                            ii = te.matmul(psV(t)[:, nh * 512:(nh + 1) * 512],
                                           onesr[:, 0:P],
                                           bvr[:, nh * 512:(nh + 1) * 512],
                                           start=False, stop=True)
                        I(ii, "pe", key=f"aP3_{t}")
                        if t >= 1:
                            emit_p4(t - 1)
                    W(te, f"w_wq{NCH - 1}")
                    for g in range(4):
                        pqb = psA[:, (g % 2) * 512:(g % 2) * 512 + 512]
                        if g >= 2:
                            W(te, f"aVQg_{st}_{g - 2}")
                        elif g == 0:
                            W(te, f"aV5_{2 * st}")
                            W(te, f"aA4_{2 * st + 1}")
                        for cc_ in range(2):
                            c = 2 * g + cc_
                            pq = pqb[:, cc_ * 256:cc_ * 256 + 256]
                            te.matmul(pq, bqr[:, c * P:(c + 1) * P],
                                      onesr[:, 0:256],
                                      start=(cc_ == 0), stop=False)
                            for w in range(NCH):
                                ii = te.matmul(pq,
                                               wq_v[:, w, c * P:(c + 1) * P],
                                               srcTs[:, w, :], start=False,
                                               stop=(w == NCH - 1
                                                     and cc_ == 1))
                            if cc_ == 1:
                                I(ii, "pe", key=f"aPQg_{st}_{g}")
                emit_p4(T - 1)

        if phases < 2:
            drain_tail()
            return nc, ct
        with nc.Block("finalize", no_gpsimd_drain=True) as blk:
            @blk.vector
            def _(v):
                v = serial(v, "dv")
                v.tensor_copy(kvsb[:, :], psC[0:DH, :], key="fkv")
                v.tensor_copy(ksb[:, :], psD[0:DH, 0:H], key="fks")
                W(v, "fpad")
                v.tensor_copy(kvz[:, :], psA[:, :], key="fkvz")
                for h in range(H):
                    colz = (h % 2) * 8 + (h // 2)
                    v.tensor_copy(ksz[:, h // 2, h:h + 1],
                                  psB[:, colz:colz + 1],
                                  key=("fksz" if h == H - 1 else None))

            @blk.tensor
            def _(te):
                W(te, "fks")
                for h in range(H):
                    col = (h % 2) * 512 + (h // 2) * DH
                    pad = iplo if h % 2 == 0 else iphi
                    te.matmul(psA[:, h * DH:(h + 1) * DH], pad[:, :],
                              kvsb[:, col:col + DH],
                              start=(h in (0, 8)), stop=(h in (7, 15)))
                te.matmul(psB[:, 0:8], iplo[:, :], ksb[:, 0:8], start=True,
                          stop=False)
                ii = te.matmul(psB[:, 8:16], iphi[:, :], ksb[:, 8:16],
                               start=False, stop=True)
                I(ii, "pe", key="fpad")

        if phases < 3:
            drain_tail()
            return nc, ct
        with nc.Block("passB1", no_gpsimd_drain=True) as blk:
            srcb1 = [scr1, scr2]
            zcolsP = [zcols, zcols2]

            def psO(t):
                return psA

            def psZ(t):
                return psD[:, 0:H] if t % 2 == 0 else psD[:, H:2 * H]

            @blk.sync
            def _(sy):
                for t in range(T):
                    W(sy, f"bPo_{t - 2}")
                    sy.dma_start(qtbuf[t % 2][:, :, :],
                                 qt_d[t].rearrange("p (c s) -> p c s", c=NCH)
                                 ).then_inc(sem[f"qtl{t % 2}"], 16)
                    ct.inc(f"qtl{t % 2}", 16, key=f"bqtl_{t}")
                    W(sy, f"bVsrc_{t - 2}")
                    sy.dma_start(xbuf[t % 2][:, :], x_d[t * P:(t + 1) * P, :]
                                 ).then_inc(sem[f"qx{t % 2}"], 16)
                    ct.inc(f"qx{t % 2}", 16, key=f"bx_{t}")


            @blk.gpsimd
            def _(gp):
                def ptt(u):
                    W(gp, f"bVx1v_{u}")
                    ii = gp.tensor_tensor(stf32[u % 2][:, :],
                                          stf32[u % 2][:, :], be1r[:, :],
                                          ALU.add)
                    I(ii, "po", key=f"bVx1_{u}")
                    ct.marks[f"bG1_{u}"] = ct.marks[f"bVx1_{u}"]

                for t in range(T):
                    if t >= 2:
                        ptt(t - 2)
                    if t % 3 == 0 and t // 3 < NCH:
                        w = t // 3
                        W(gp, f"bVt_{min(3 * w, T - 1)}")
                        gp.dma_start(w1_v[w][:, :],
                                     w1_d[w * P:(w + 1) * P, :]
                                     ).then_inc(sem["qg_w1"], 16)
                        ct.inc("qg_w1", 16, key=f"w_w1{w}")
                ptt(T - 2)
                ptt(T - 1)
                W(gp, "bst1_0")
                W(gp, f"bVtq_{T - 1}")
                gp.dma_start(x1bf[:, :], x1_d[0]).then_inc(sem["qg_xc0"], 16)
                ct.inc("qg_xc0", 16, key="cVcst_0")
                W(gp, "bst1_1")
                gp.dma_start(x1bfB[:, :], x1_d[1]).then_inc(sem["qg_xc1"], 16)
                ct.inc("qg_xc1", 16, key="cVcst_1")
                W(gp, f"bVt_{T - 1}")
                gp.dma_start(wfstage[:, :], wf_d[0:P, :]
                             ).then_inc(sem["qg_wf"], 16)
                ct.inc("qg_wf", 16, key="w_wf0")

            @blk.tensor
            def _(te):
                def stage_head(t):
                    zps = psZ(t)
                    W(te, f"bqtl_{t}")
                    W(te, f"bVz2_{t - 2}")
                    if t == 0:
                        W(te, "fksz")
                    if t == 1:
                        W(te, "fkv")
                    for c in range(NCH):
                        te.matmul(zps, qtbuf[t % 2][:, c, :], ksz[:, c, :],
                                  start=(c == 0), stop=False)
                    ii = te.matmul(zps, onesr[:, 0:P], epszr[:, :],
                                   start=False, stop=True)
                    I(ii, "pe", key=f"bPZ_{t}")
                    W(te, f"bAs_{t - 1}")
                    if t == 0:
                        W(te, "fkvz")
                    for h in range(H):
                        ii = te.matmul(psO(t)[:, h * DH:(h + 1) * DH],
                                       qtbuf[t % 2][:, h // 2, :],
                                       kvz[:, h * DH:(h + 1) * DH],
                                       start=(h in (0, 8)), stop=(h in (7, 15)))
                    I(ii, "pe", key=f"bPo_{t}")

                def pe_PoT(u):
                    W(te, f"bAs_{u}")
                    W(te, f"bVoT_{u - 1}")
                    for c in range(NCH):
                        ii = te.transpose(ps_bf[:, c * P:(c + 1) * P],
                                          obf[:, c * P:(c + 1) * P],
                                          identb[:, :])
                    I(ii, "pe", key=f"bPoT_{u}")

                def pe_Pat(u):
                    pat = psB if u % 2 == 0 else psC
                    W(te, f"bVoT_{u}")
                    W(te, f"bVt_{u - 2}")
                    if u == 1:
                        W(te, "fkv")
                    W(te, f"w_wo{NCH - 1}")
                    for w in range(NCH):
                        for nh in range(2):
                            te.matmul(pat[:, nh * 512:(nh + 1) * 512],
                                      oTv[:, w * P:(w + 1) * P],
                                      wo_v[:, w, nh * 512:(nh + 1) * 512],
                                      start=(w == 0), stop=False)
                    for nh in range(2):
                        ii = te.matmul(pat[:, nh * 512:(nh + 1) * 512],
                                       onesr[:, 0:P],
                                       bor[:, nh * 512:(nh + 1) * 512],
                                       start=False, stop=True)
                    I(ii, "pe", key=f"bPat_{u}")

                for t in range(T):
                    if t >= 1:
                        pe_PoT(t - 1)
                    stage_head(t)
                    if t >= 1:
                        pe_Pat(t - 1)
                pe_PoT(T - 1)
                pe_Pat(T - 1)

            @blk.vector
            def _(v):
                v = serial(v, "dv")
                bcol = [(4, 5, 6, 7, 8, 9), (10, 11, 12, 13, 14, 15)]

                def vec_head(t):
                    W(v, f"bx_{t}")
                    W(v, f"bVt_{t - 2}")
                    v.tensor_scalar(srcb1[t % 2][:, :], xbuf[t % 2][:, :],
                                    m0c[:, t:t + 1], r0c[:, t:t + 1],
                                    ALU.subtract, ALU.mult, key=f"bVsrc_{t}")
                    W(v, f"bPZ_{t}")
                    W(v, f"bAs_{t - 2}")
                    v.reciprocal(zcolsP[t % 2][:, :], psZ(t),
                                 key=f"bVz2_{t}")
                    W(v, f"bPo_{t}")
                    W(v, f"bPoT_{t - 1}")
                    W(v, f"bVtq_{t - 2}")
                    zc = zcolsP[t % 2]
                    zb = bass.AP(zc.tensor, zc.offset,
                                 [list(zc.ap[0]), [1, H], [0, DH]])
                    v.tensor_tensor(
                        obf[:, :].rearrange("p (h j) -> p h j", h=H),
                        psO(t)[:, :].rearrange("p (h j) -> p h j", h=H),
                        zb, ALU.mult, key=f"bAs_{t}")

                def vec_t(u):
                    cs = bcol[u % 2]
                    pat = psB if u % 2 == 0 else psC
                    W(v, f"bPat_{u}")
                    W(v, f"bVtq_{u - 1}")
                    v.scalar_tensor_tensor(tf32[:, :], pat[:, :], 1.0,
                                           srcb1[u % 2][:, :], ALU.mult,
                                           ALU.add,
                                           accum_out=cc[:, cs[0]:cs[0] + 1],
                                           key=f"bVt_{u}")

                def vec_x1(u):
                    cs = bcol[u % 2]
                    W(v, f"bVtq_{u}")
                    v.tensor_scalar(cc[:, cs[2]:cs[2] + 1],
                                    cc[:, cs[0]:cs[0] + 1], INV_D, None,
                                    ALU.mult)
                    v.tensor_tensor(cc[:, cs[3]:cs[3] + 1],
                                    cc[:, cs[2]:cs[2] + 1],
                                    cc[:, cs[2]:cs[2] + 1], ALU.mult)
                    v.tensor_scalar(cc[:, cs[4]:cs[4] + 1],
                                    cc[:, cs[1]:cs[1] + 1], INV_D,
                                    cc[:, cs[3]:cs[3] + 1], ALU.mult,
                                    ALU.subtract, key=f"bVc_{u}")
                    W(v, f"bAr1_{u}")
                    if u < 2:
                        W(v, f"aqts_{NSUP - 1}_1")
                    W(v, f"bG1_{u - 2}")
                    W(v, f"bst1_{u - 2}")
                    v.scalar_tensor_tensor(stf32[u % 2][:, :], tf32[:, :],
                                           cc[:, cs[2]:cs[2] + 1], g1r[:, :],
                                           ALU.subtract, ALU.mult)
                    v.tensor_scalar(stf32[u % 2][:, :], stf32[u % 2][:, :],
                                    cc[:, cs[5]:cs[5] + 1], None, ALU.mult,
                                    key=f"bVx1v_{u}")

                for t in range(T):
                    vec_head(t)
                    if t >= 2:
                        vec_x1(t - 2)
                    if t >= 1:
                        vec_t(t - 1)
                vec_x1(T - 2)
                vec_t(T - 1)
                vec_x1(T - 1)

            @blk.scalar
            def _(sc):
                raw = sc
                sc = serial(sc, "ac")
                bcol = [(4, 5, 6, 7, 8, 9), (10, 11, 12, 13, 14, 15)]

                def sc_VoT(u):
                    W(sc, f"bPoT_{u}")
                    sc.copy(oTv[:, :], ps_bf[:, :], key=f"bVoT_{u}")

                def sc_Sq(u):
                    cs = bcol[u % 2]
                    W(sc, f"bVt_{u}")
                    sc.activation(obf[:, :], tf32[:, :], AF.Square,
                                  accum_out=cc[:, cs[1]:cs[1] + 1],
                                  key=f"bVtq_{u}")
                    W(sc, f"bVc_{u}")
                    sc.activation(cc[:, cs[5]:cs[5] + 1],
                                  cc[:, cs[4]:cs[4] + 1], AF.Ln,
                                  bias=EPS_LN)
                    sc.activation(cc[:, cs[5]:cs[5] + 1],
                                  cc[:, cs[5]:cs[5] + 1], AF.Exp,
                                  scale=-0.5, key=f"bAr1_{u}")

                def sc_store(u):
                    W(raw, f"bG1_{u}")
                    raw.dma_start(x1_d[u], stf32[u % 2][:, :]
                                  ).then_inc(sem[f"qs{u % 2}"], 16)
                    ct.inc(f"qs{u % 2}", 16, key=f"bst1_{u}")

                for t in range(T):
                    if t >= 1:
                        sc_VoT(t - 1)
                    if t >= 2:
                        sc_Sq(t - 2)
                    if t >= 3:
                        sc_store(t - 3)
                sc_VoT(T - 1)
                sc_Sq(T - 2)
                sc_Sq(T - 1)
                for u in (T - 3, T - 2, T - 1):
                    sc_store(u)

        if phases < 4:
            drain_tail()
            return nc, ct
        with nc.Block("passB2a", no_gpsimd_drain=True) as blk:
            @blk.sync
            def _(sy):
                sy.dma_start(g2r[:, :], g2_d).then_inc(sem["qw"], 16)
                ct.inc("qw", 16)
                sy.dma_start(be2r[:, :], be2_d).then_inc(sem["qw"], 16)
                ct.inc("qw", 16, key="w_g2")
                for t in range(T):
                    W(sy, f"cVuP_{t - 3}")
                    W(sy, f"bst1_{t}")
                    sy.dma_start(ldf3[t % 3][:, :], x1_d[t]
                                 ).then_inc(sem[f"ql{t % 3}"], 16)
                    ct.inc(f"ql{t % 3}", 16, key=f"clx1_{t}")


            @blk.gpsimd
            def _(gp):
                def xc(t):
                    if t >= T:
                        return
                    W(gp, f"bst1_{t}")
                    W(gp, f"cPt_{t - 2}")
                    xb = x1bf if t % 2 == 0 else x1bfB
                    gp.dma_start(xb[:, :], x1_d[t]
                                 ).then_inc(sem[f"qg_xc{t % 2}"], 16)
                    ct.inc(f"qg_xc{t % 2}", 16, key=f"cVcst_{t}")

                gp.dma_start(dshb[:, :], dsh_d).then_inc(sem["qg_dsh"], 16)
                ct.inc("qg_dsh", 16, key="w_dsh")
                for st in range(NSUP):
                    if 1 <= st < NCH:
                        W(gp, f"cVW_{st - 1}")
                        gp.dma_start(wfstage[:, :],
                                     wf_d[st * P:(st + 1) * P, :]
                                     ).then_inc(sem["qg_wf"], 16)
                        ct.inc("qg_wf", 16, key=f"w_wf{st}")
                    xc(2 * st + 2)
                    xc(2 * st + 3)

            @blk.tensor
            def _(te):
                psDbf = psD.ap().bitcast(BF16)
                for st in range(NSUP):
                    for sub in range(2):
                        t = 2 * st + sub
                        pt = ps_bf[:, :] if sub == 0 else psDbf[:, 0:1024]
                        xb = x1bf if t % 2 == 0 else x1bfB
                        W(te, f"cVcst_{t}")
                        W(te, f"cVxT_{t - 2}")
                        if sub == 0:
                            W(te, f"cVW_{st - 1}")
                        for c in range(NCH):
                            ii = te.transpose(pt[:, c * P:(c + 1) * P],
                                              xb[:, c * P:(c + 1) * P],
                                              identb[:, :])
                        I(ii, "pe", key=f"cPt_{t}")
                    if st < NCH:
                        W(te, f"w_wf{st}")
                        W(te, f"cVxT_{2 * st}")
                        for c in range(NCH):
                            ii = te.transpose(ps_bf[:, c * P:(c + 1) * P],
                                              wfstage[:, c * P:(c + 1) * P],
                                              identb[:, :])
                        I(ii, "pe", key=f"cPW_{st}")
                    W(te, f"cVxT_{2 * st + 1}")
                    if st == 0:
                        W(te, f"w_w1{NCH - 1}")
                        W(te, f"w_w2{NFCH - 1}")
                    x1t = x1Ts if st % 2 == 0 else x1Ts2

                    def emit_ph(c):
                        ph = psC[:, 0:256] if c % 2 == 0 else psC[:, 512:768]
                        if c >= 2:
                            W(te, f"cAg_{st}_{c - 2}")
                        else:
                            W(te, f"cAg_{st - 1}_{NFCH - 2 + c}")
                        for w in range(NCH):
                            ii = te.matmul(ph, w1_v[w][:, c * P:(c + 1) * P],
                                           x1t[:, w, :], start=(w == 0),
                                           stop=(w == NCH - 1))
                        I(ii, "pe", key=f"cPh_{st}_{c}")

                    def emit_py(c):
                        W(te, f"cAg_{st}_{c}")
                        for sub in range(2):
                            psY = psA if sub == 0 else psB
                            if c == 0:
                                W(te, f"cVuP_{2 * (st - 1) + sub}")
                            for nh in range(2):
                                ii = te.matmul(
                                    psY[:, nh * 512:(nh + 1) * 512],
                                    hT[c % 2][:, sub * P:(sub + 1) * P],
                                    w2_v[c][:, nh * 512:(nh + 1) * 512],
                                    start=(c == 0), stop=False)
                        I(ii, "pe", key=f"cPy_{st}_{c}")

                    for c in range(NFCH):
                        emit_ph(c)
                        if c >= 1:
                            emit_py(c - 1)
                    emit_py(NFCH - 1)
                    for sub in range(2):
                        psY = psA if sub == 0 else psB
                        for nh in range(2):
                            ii = te.matmul(psY[:, nh * 512:(nh + 1) * 512],
                                           onesr[:, 0:P],
                                           b2r[:, nh * 512:(nh + 1) * 512],
                                           start=False, stop=True)
                        I(ii, "pe", key=f"cPyb_{st}_{sub}")

            @blk.vector
            def _(v):
                v = serial(v, "dv")

                psDbf_v = psD.ap().bitcast(BF16)

                def cvw(st):
                    pass

                def cvu(t):
                    st = t // 2
                    sub = t % 2
                    psY = psA if sub == 0 else psB
                    W(v, f"cPyb_{st}_{sub}")
                    W(v, f"clx1_{t}")
                    if t < 2:
                        W(v, f"bst1_{T - 2 + t}")
                    W(v, f"cstu_{t - 2}")
                    W(v, f"cSq_{t - 2}")
                    v.scalar_tensor_tensor(stf32[t % 2][:, :], psY[:, :],
                                           1.0, ldf3[t % 3][:, :],
                                           ALU.mult, ALU.add,
                                           accum_out=usc[:, t:t + 1],
                                           key=f"cVuP_{t}")

                for st in range(NSUP):
                    if st >= 1:
                        cvu(2 * st - 2)
                        cvu(2 * st - 1)
                    cvw(st)
                cvu(T - 2)
                cvu(T - 1)

            @blk.scalar
            def _(sc):
                raw = sc
                sc = serial(sc, "ac")


                def scw(st):
                    if st < NCH:
                        W(sc, f"cPW_{st}")
                        sc.copy(wfT[:, :, st * P:(st + 1) * P],
                                ps_bf[:, :].rearrange("p (c s) -> p c s",
                                                      c=NCH),
                                key=f"cVW_{st}")

                def scxt(t):
                    st_ = t // 2
                    sub = t % 2
                    pt = ps_bf[:, :] if sub == 0 else psDbf_s[:, 0:1024]
                    W(sc, f"cPt_{t}")
                    W(sc, f"cPh_{st_ - 2}_{NFCH - 1}")
                    x1tv = x1Ts if st_ % 2 == 0 else x1Ts2
                    sc.copy(x1tv[:, :, sub * P:(sub + 1) * P],
                            pt.rearrange("p (c s) -> p c s", c=NCH),
                            key=f"cVxT_{t}")

                psDbf_s = psD.ap().bitcast(BF16)
                for st in range(NSUP):
                    scxt(2 * st)
                    scw(st)
                    scxt(2 * st + 1)
                    for c in range(NFCH):
                        ph = psC[:, 0:256] if c % 2 == 0 else psC[:, 512:768]
                        W(sc, f"cPh_{st}_{c}")
                        if c >= 2:
                            W(sc, f"cPy_{st}_{c - 2}")
                        else:
                            W(sc, f"cPy_{st - 1}_{NFCH - 2 + c}")
                        sc.activation(hT[c % 2][:, :], ph, AF.Gelu,
                                        bias=b1c[:, c:c + 1], key=f"cAg_{st}_{c}")
                    for sub in range(2):
                        t = 2 * st + sub
                        W(sc, f"cVuP_{t}")
                        sc.activation(x1bf2[:, :], stf32[t % 2][:, :],
                                      AF.Square, accum_out=uqc[:, t:t + 1],
                                      key=f"cSq_{t}")
                        W(raw, f"cSq_{t}")
                        raw.dma_start(u_d[t], stf32[t % 2][:, :]
                                      ).then_inc(sem[f"qs{t % 2}"], 16)
                        ct.inc(f"qs{t % 2}", 16, key=f"cstu_{t}")

        if phases < 5:
            drain_tail()
            return nc, ct
        with nc.Block("passB2b", no_gpsimd_drain=True) as blk:
            psBbf = psB.ap().bitcast(BF16)
            scrAP = [scrA, scrA2]
            xAvP = [xAv, xAv2]
            nbfP = [nbf, nbf2]
            nTP = [nTv, nT2]
            poolTP = [poolT, poolT2]
            xAv4P = [xAv, xAv2, xAv3, xAv4]

            def i4(t):
                return ((t // 2) % 2) * 2 + (t % 2)
            colset = [(4, 5, 14, 15, 7, 12), (0, 1, 2, 3, 10, 11)]

            @blk.sync
            def _(sy):
                for t in range(T):
                    W(sy, f"dVg2_{t - 2}")
                    W(sy, f"cstu_{t}")
                    sy.dma_start(ldf32[t % 2][:, :], u_d[t]
                                 ).then_inc(sem[f"ql{t % 2}"], 16)
                    ct.inc(f"ql{t % 2}", 16, key=f"dlu_{t}")
                    W(sy, f"dVxA_{t - 2}")
                    sy.dma_start(xbuf[t % 2][:, :], x_d[t * P:(t + 1) * P, :]
                                 ).then_inc(sem[f"qx{t % 2}"], 16)
                    ct.inc(f"qx{t % 2}", 16, key=f"dx_{t}")

            @blk.vector
            def _(v):
                v = serial(v, "dv")
                v.tensor_scalar(m0c[:, :], usc[:, :], INV_D, None, ALU.mult)
                v.tensor_tensor(usc[:, :], m0c[:, :], m0c[:, :], ALU.mult)
                v.tensor_scalar(uqc[:, :], uqc[:, :], INV_D, None, ALU.mult)
                v.tensor_tensor(uqc[:, :], uqc[:, :], usc[:, :], ALU.subtract,
                                key="dVln2")

                def vg2(t):
                    cs = colset[t % 2]
                    if t == 0:
                        W(v, "w_g2")
                        W(v, "dAr2")
                    W(v, f"dlu_{t}")
                    v.scalar_tensor_tensor(scrAP[t % 2][:, :],
                                           ldf32[t % 2][:, :],
                                           m0c[:, t:t + 1], g2r[:, :],
                                           ALU.subtract, ALU.mult,
                                           key=f"dVg2_{t}")

                def vcols(u):
                    cs = colset[u % 2]
                    W(v, f"dSq_{u}")
                    v.tensor_scalar(cc[:, cs[2]:cs[2] + 1],
                                    cc[:, cs[0]:cs[0] + 1], INV_D, None,
                                    ALU.mult)
                    v.tensor_tensor(cc[:, cs[4]:cs[4] + 1],
                                    cc[:, cs[2]:cs[2] + 1],
                                    cc[:, cs[2]:cs[2] + 1], ALU.mult)
                    v.tensor_scalar(cc[:, cs[3]:cs[3] + 1],
                                    cc[:, cs[1]:cs[1] + 1], INV_D,
                                    cc[:, cs[4]:cs[4] + 1], ALU.mult,
                                    ALU.subtract, key=f"dVc3_{u}")

                def vxa(t):
                    cs = colset[t % 2]
                    W(v, f"dXbe_{t}")
                    v.scalar_tensor_tensor(xAv4P[i4(t)][:, :],
                                           scrAP[t % 2][:, :],
                                           r0c[:, t:t + 1],
                                           xbuf[t % 2][:, :], ALU.mult,
                                           ALU.add,
                                           accum_out=cc[:, cs[0]:cs[0] + 1],
                                           key=f"dVxA_{t}")

                def vn(u):
                    cs = colset[u % 2]
                    W(v, f"dAr3_{u}")
                    W(v, f"dPn_{u - 2}")
                    v.tensor_scalar(nbfP[u % 2][:, :], xAv4P[i4(u)][:, :],
                                    cc[:, cs[2]:cs[2] + 1],
                                    cc[:, cs[5]:cs[5] + 1], ALU.subtract,
                                    ALU.mult, key=f"dVn_{u}")

                def vout(u):
                    ps = psA if u % 2 == 0 else psC
                    W(v, f"dPr_{u}")
                    if u < 2:
                        W(v, f"cstu_{T - 2 + u}")
                    W(v, f"dsto_{u - 2}")
                    v.tensor_tensor(stf32[u % 2][:, :], ps[:, :],
                                    xAv4P[i4(u)][:, :], ALU.add,
                                    key=f"dVout_{u}")

                for p in range(T // 2):
                    t, t2 = 2 * p, 2 * p + 1
                    vg2(t)
                    vg2(t2)
                    if p >= 1:
                        vcols(t - 2)
                        vcols(t2 - 2)
                    if p >= 2:
                        vout(t - 4)
                        vout(t2 - 4)
                    if p >= 1:
                        vn(t - 2)
                        vn(t2 - 2)
                    vxa(t)
                    vxa(t2)
                vcols(T - 2)
                vcols(T - 1)
                vout(T - 4)
                vout(T - 3)
                vn(T - 2)
                vn(T - 1)
                vout(T - 2)
                vout(T - 1)

            @blk.scalar
            def _(sc):
                raw = sc
                sc = serial(sc, "ac")
                W(sc, "dVln2")
                sc.activation(r0c[:, :], uqc[:, :], AF.Ln, bias=EPS_LN)
                sc.activation(r0c[:, :], r0c[:, :], AF.Exp, scale=-0.5,
                              key="dAr2")
                def ar3(u):
                    cs = colset[u % 2]
                    W(sc, f"dVc3_{u}")
                    sc.activation(cc[:, cs[5]:cs[5] + 1],
                                  cc[:, cs[3]:cs[3] + 1], AF.Ln,
                                  bias=EPS_LN)
                    sc.activation(cc[:, cs[5]:cs[5] + 1],
                                  cc[:, cs[5]:cs[5] + 1], AF.Exp,
                                  scale=-0.5, key=f"dAr3_{u}")

                def nT_(u):
                    pn = ps_bf[:, :] if u % 2 == 0 else psBbf[:, 0:1024]
                    W(sc, f"dPr_{u - 2}")
                    W(sc, f"dPn_{u}")
                    sc.copy(nTP[u % 2][:, :, :],
                            pn.rearrange("p (c s) -> p c s", c=NCH),
                            key=f"dVnT_{u}")

                def pool_(u):
                    pp = psD[0:DH, 0:P] if u % 2 == 0 \
                        else psB[0:DH, 512:640]
                    W(sc, f"dPp_{u}")
                    sc.copy(poolTP[u % 2][:, :], pp, key=f"dVpo_{u}")

                def sq(t):
                    cs = colset[t % 2]
                    W(sc, f"dVxA_{t}")
                    sc.activation(nbfP[t % 2][:, :], xAv4P[i4(t)][:, :],
                                  AF.Square,
                                  accum_out=cc[:, cs[1]:cs[1] + 1],
                                  key=f"dSq_{t}")

                def sto(u):
                    W(raw, f"dVout_{u}")
                    raw.dma_start(out_d[u * P:(u + 1) * P, :],
                                  stf32[u % 2][:, :]
                                  ).then_inc(sem[f"qs{u % 2}"], 16)
                    ct.inc(f"qs{u % 2}", 16, key=f"dsto_{u}")

                for p in range(T // 2):
                    t, t2 = 2 * p, 2 * p + 1
                    if p >= 1:
                        ar3(t - 2)
                        ar3(t2 - 2)
                        nT_(t - 2)
                        nT_(t2 - 2)
                        pool_(t - 2)
                        pool_(t2 - 2)
                    if p >= 2:
                        sto(t - 4)
                        sto(t2 - 4)
                    sq(t)
                    sq(t2)
                ar3(T - 2)
                ar3(T - 1)
                nT_(T - 2)
                nT_(T - 1)
                pool_(T - 2)
                pool_(T - 1)
                for u in (T - 4, T - 3, T - 2, T - 1):
                    sto(u)

            @blk.gpsimd
            def _(gp):
                sgp = serial(gp, "po")
                for t in range(T):
                    W(gp, f"dx_{t}")
                    if t == 0:
                        W(gp, "w_g2")
                    sgp.tensor_tensor(xbuf[t % 2][:, :], xbuf[t % 2][:, :],
                                      be2r[:, :], ALU.add, key=f"dXbe_{t}")

            @blk.tensor
            def _(te):
                def pn_(u):
                    pn = ps_bf[:, :] if u % 2 == 0 else psBbf[:, 0:1024]
                    W(te, f"dVn_{u}")
                    W(te, f"dVnT_{u - 2}")
                    for c in range(NCH):
                        ii = te.transpose(pn[:, c * P:(c + 1) * P],
                                          nbfP[u % 2][:, c * P:(c + 1) * P],
                                          identb[:, :])
                    I(ii, "pe", key=f"dPn_{u}")

                def pp_(u):
                    pp = psD[0:DH, 0:P] if u % 2 == 0 else psB[0:DH, 512:640]
                    W(te, f"dVnT_{u}")
                    W(te, f"dVpo_{u - 2}")
                    for c in range(NCH):
                        ii = te.matmul(pp, istk[:, :], nTP[u % 2][:, c, :],
                                       start=(c == 0), stop=(c == NCH - 1))
                    I(ii, "pe", key=f"dPp_{u}")

                def pr_(u):
                    ps = psA if u % 2 == 0 else psC
                    W(te, f"dVout_{u - 2}")
                    if u == 0:
                        W(te, "w_dsh")
                        W(te, f"cVW_{NCH - 1}")
                    for w in range(NCH):
                        for nh in range(2):
                            te.matmul(ps[:, nh * 512:(nh + 1) * 512],
                                      nTP[u % 2][:, w, :],
                                      wfT[:, w, nh * 512:(nh + 1) * 512],
                                      start=(w == 0), stop=False)
                    W(te, f"dVpo_{u}")
                    for nh in range(2):
                        ii = te.matmul(ps[:, nh * 512:(nh + 1) * 512],
                                       poolTP[u % 2][:, :],
                                       dshb[:, nh * 512:(nh + 1) * 512],
                                       start=False, stop=True)
                    I(ii, "pe", key=f"dPr_{u}")

                for p in range(1, T // 2):
                    t, t2 = 2 * p, 2 * p + 1
                    pn_(t - 2)
                    pn_(t2 - 2)
                    pp_(t - 2)
                    pp_(t2 - 2)
                    pr_(t - 2)
                    pr_(t2 - 2)
                pn_(T - 2)
                pn_(T - 1)
                pp_(T - 2)
                pp_(T - 1)
                pr_(T - 2)
                pr_(T - 1)

        drain_tail()
    return nc, ct


def build_full(debug=False, phases=5):
    _, ct1 = build_nc(debug=debug, phases=phases)
    nc, _ = build_nc(debug=debug, phases=phases, marks=ct1.marks)
    return nc


_DBG = [False, 5]


def kernel(**inputs):
    x = np.asarray(inputs["L0_attn_src"], dtype=np.float32)
    B = x.shape[0]
    w = {k: np.asarray(v, dtype=np.float32) for k, v in inputs.items()}

    ident = np.eye(P, dtype=np.float32)
    shared = {
        "Wq": w["L0_attn_Wq"], "Wk": w["L0_attn_Wk"], "Wv": w["L0_attn_Wv"],
        "Wo": w["L0_attn_Wo"], "W1": w["L0_attn_W1"], "W2": w["L0_attn_W2"],
        "Wf": w["L0_ffn_weight"], "disha": w["L0_ffn_disha"],
        "bq": w["L0_attn_bq"].reshape(1, D),
        "bk": w["L0_attn_bk"].reshape(1, D),
        "bv": w["L0_attn_bv"].reshape(1, D),
        "bo": w["L0_attn_bo"].reshape(1, D),
        "b1c": np.ascontiguousarray(
            w["L0_attn_b1"].reshape(NFCH, P).T),
        "b2": w["L0_attn_b2"].reshape(1, D),
        "g1r": np.ascontiguousarray(np.broadcast_to(w["L0_attn_g1"], (P, D))),
        "be1r": np.ascontiguousarray(np.broadcast_to(w["L0_attn_be1"], (P, D))),
        "g2r": np.ascontiguousarray(np.broadcast_to(w["L0_attn_g2"], (P, D))),
        "be2r": np.ascontiguousarray(np.broadcast_to(w["L0_attn_be2"], (P, D))),
        "identb": ident.astype(BF),
        "identf": ident,
        "ipadlo": np.concatenate([np.eye(DH), np.zeros((DH, DH))],
                                 axis=1).astype(BF),
        "ipadhi": np.concatenate([np.zeros((DH, DH)), np.eye(DH)],
                                 axis=1).astype(BF),
        "istack": np.concatenate([np.eye(DH), np.eye(DH)], axis=0).astype(BF),
        "onesr": np.ones((1, 512), dtype=np.float32).astype(BF),
        "onesc": np.ones((P, 1), dtype=np.float32).astype(BF),
        "epsz": np.full((1, H), EPS_Z, dtype=np.float32).astype(BF),
    }

    nc = build_full(debug=_DBG[0], phases=_DBG[1])
    in_maps = []
    for b in range(B):
        m = dict(shared)
        m["x"] = np.ascontiguousarray(x[b])
        in_maps.append(m)
    res = run_bass_kernel_spmd(nc, in_maps, core_ids=list(range(B)))
    out = np.stack([np.asarray(r["out"]) for r in res.results], axis=0)
    return out.astype(np.float32)


# version 33
# speedup vs baseline: 1.0234x; 1.0234x over previous
import contextlib
import numpy as np
import ml_dtypes
import concourse.bass as bass
import concourse.mybir as mybir
from concourse.bass_utils import run_bass_kernel_spmd

F32 = mybir.dt.float32
BF16 = mybir.dt.bfloat16
U8 = mybir.dt.uint8
AF = mybir.ActivationFunctionType
ALU = mybir.AluOpType

S, D, P, T, H, DH, FF = 4096, 1024, 128, 32, 16, 64, 4096
NCH = D // P
NFCH = FF // P
NSUP = T // 2
INV_D = 1.0 / D
EPS_LN = 1e-5
EPS_Z = 1e-6
BF = np.dtype(ml_dtypes.bfloat16)


class Ctr:
    def __init__(self):
        self.c = {}
        self.marks = {}

    def inc(self, name, n=1, key=None):
        self.c[name] = self.c.get(name, 0) + n
        if key is not None:
            self.marks[key] = (name, self.c[name])
        return self.c[name]


def build_nc(debug=False, phases=5, marks=None):
    nc = bass.Bass("TRN2", target_bir_lowering=False)

    x_d = nc.dram_tensor("x", [S, D], F32, kind="ExternalInput").ap()
    wq_d = nc.dram_tensor("Wq", [D, D], F32, kind="ExternalInput").ap()
    wk_d = nc.dram_tensor("Wk", [D, D], F32, kind="ExternalInput").ap()
    wv_d = nc.dram_tensor("Wv", [D, D], F32, kind="ExternalInput").ap()
    wo_d = nc.dram_tensor("Wo", [D, D], F32, kind="ExternalInput").ap()
    w1_d = nc.dram_tensor("W1", [D, FF], F32, kind="ExternalInput").ap()
    w2_d = nc.dram_tensor("W2", [FF, D], F32, kind="ExternalInput").ap()
    wf_d = nc.dram_tensor("Wf", [D, D], F32, kind="ExternalInput").ap()
    dsh_d = nc.dram_tensor("disha", [DH, D], F32, kind="ExternalInput").ap()
    bq_d = nc.dram_tensor("bq", [1, D], F32, kind="ExternalInput").ap()
    bk_d = nc.dram_tensor("bk", [1, D], F32, kind="ExternalInput").ap()
    bv_d = nc.dram_tensor("bv", [1, D], F32, kind="ExternalInput").ap()
    bo_d = nc.dram_tensor("bo", [1, D], F32, kind="ExternalInput").ap()
    b1_d = nc.dram_tensor("b1c", [P, NFCH], F32, kind="ExternalInput").ap()
    b2_d = nc.dram_tensor("b2", [1, D], F32, kind="ExternalInput").ap()
    g1_d = nc.dram_tensor("g1r", [P, D], F32, kind="ExternalInput").ap()
    be1_d = nc.dram_tensor("be1r", [P, D], F32, kind="ExternalInput").ap()
    g2_d = nc.dram_tensor("g2r", [P, D], F32, kind="ExternalInput").ap()
    be2_d = nc.dram_tensor("be2r", [P, D], F32, kind="ExternalInput").ap()
    idb_d = nc.dram_tensor("identb", [P, P], BF16, kind="ExternalInput").ap()
    idf_d = nc.dram_tensor("identf", [P, P], F32, kind="ExternalInput").ap()
    iplo_d = nc.dram_tensor("ipadlo", [DH, P], BF16, kind="ExternalInput").ap()
    iphi_d = nc.dram_tensor("ipadhi", [DH, P], BF16, kind="ExternalInput").ap()
    istk_d = nc.dram_tensor("istack", [P, DH], BF16, kind="ExternalInput").ap()
    ones_d = nc.dram_tensor("onesr", [1, 512], BF16, kind="ExternalInput").ap()
    onec_d = nc.dram_tensor("onesc", [P, 1], BF16, kind="ExternalInput").ap()
    epsz_d = nc.dram_tensor("epsz", [1, H], BF16, kind="ExternalInput").ap()

    out_d = nc.dram_tensor("out", [S, D], F32, kind="ExternalOutput").ap()
    skind = "ExternalOutput" if debug else "Internal"
    qt_d = nc.dram_tensor("qt_s", [T, P, D], BF16, kind=skind).ap()
    x1_d = nc.dram_tensor("x1_s", [T, P, D], F32, kind=skind).ap()
    u_d = nc.dram_tensor("u_s", [T, P, D], F32, kind=skind).ap()

    ARENA = 212736
    es = contextlib.ExitStack()
    with es:
        arena = es.enter_context(nc.sbuf_tensor("arena", [P, ARENA], U8))
        av = arena.ap()

        def VF(off, n, part=P, p0=0):
            return av[p0:p0 + part, off:off + 4 * n].bitcast(F32)

        def VB(off, n, part=P, p0=0):
            return av[p0:p0 + part, off:off + 2 * n].bitcast(BF16)

        xbuf = [VF(0, D), VF(4096, D)]
        qtsup = VB(8192, NCH * 256).rearrange("p (c s) -> p c s", c=NCH)
        qts4 = VB(8192, NCH * 256).rearrange("p (u c s) -> p u c s", u=2, c=NCH)
        stf32 = [VF(8192, D), VF(12288, D)]
        srcTs = VB(16384, NCH * 256).rearrange("p (c s) -> p c s", c=NCH)
        qtbuf = [VB(16384, D).rearrange("p (c s) -> p c s", c=NCH),
                 VB(18432, D).rearrange("p (c s) -> p c s", c=NCH)]
        tf32 = VF(20480, D)
        ldf32 = [VF(16384, D), VF(20480, D)]
        ldf3 = [VF(16384, D), VF(20480, D), VF(4096, D)]
        scr1 = VF(24576, D)
        scr2bf = VB(28672, D)
        x1bf2 = VB(30720, D)
        x1bfB = VB(209920, D)
        scr2 = VF(28672, D)
        kvsb = VB(24576, D, part=DH)
        x1Ts = VB(24576, NCH * 256).rearrange("p (c s) -> p c s", c=NCH)
        x1Ts2 = VB(0, NCH * 256).rearrange("p (c s) -> p c s", c=NCH)
        wfstage = VB(28672, D)
        scrA = VF(24576, D)
        scrA2 = VF(78848, D)
        xAv2 = VF(82944, D)
        nbf2 = VB(87040, D)
        nT2 = VB(89088, D).rearrange("p (c s) -> p c s", c=NCH)
        poolT2 = VB(91136, P, part=DH)
        xAv3 = VF(93184, D)
        xAv4 = VF(97280, D)
        xb3 = VF(128000, D)
        zdum = VB(132096, D)
        xAv = VF(28672, D)
        srcbf = VB(32768, D)
        srcbf2 = VB(12288, D)
        kbf = VB(34816, D)
        vbf = VB(36864, D)
        obf = VB(32768, D)
        oTv = VB(34816, D)
        kvz = VB(36864, D)
        x1bf = VB(32768, D)
        hT = [VB(34816, 256), VB(35328, 256)]
        nbf = VB(32768, D)
        nTv = VB(34816, D).rearrange("p (c s) -> p c s", c=NCH)
        dshb = VB(36864, D, part=DH)
        o6 = 38912
        identb = VB(o6, P)
        identf = VF(o6 + 256, P)
        iplo = VB(o6 + 768, P, part=DH)
        iphi = VB(o6 + 1024, P, part=DH)
        istk = VB(o6 + 1280, DH)
        onec = VB(o6 + 1408, 1)
        m0c = VF(o6 + 1472, T)
        r0c = VF(o6 + 1600, T)
        usc = VF(o6 + 1728, T)
        uqc = VF(o6 + 1856, T)
        ksz = VB(o6 + 1984, NCH * H).rearrange("p (c h) -> p c h", c=NCH)
        zscrT = VF(o6 + 2240, P, part=H)
        zcols = VF(o6 + 2752, H)
        zcols2 = VF(o6 + 2928, H)
        cc = VF(o6 + 2816, 16)
        ec = VF(o6 + 2880, 1)
        km1 = VF(o6 + 2884, 1)
        kmh = VF(o6 + 2888, 1)
        ksb = VB(o6 + 2892, H, part=DH)
        poolT = VB(o6 + 3000, P, part=DH)
        epszr = VB(o6 + 3456, H, part=1)
        nc.const_aps.register(F32, EPS_LN, ec[:, 0:1])
        nc.const_aps.register(F32, -1.0, km1[:, 0:1])
        nc.const_aps.register(F32, -0.5, kmh[:, 0:1])
        b1c = VF(o6 + 3300, NFCH)
        o7 = 43008
        bqr = VB(o7, D, part=1, p0=0)
        bkr = VB(o7 + 2048, D, part=1, p0=0)
        bvr = VB(o7 + 4096, D, part=1, p0=0)
        bor = VB(o7 + 6144, D, part=1, p0=0)
        b2r = VB(o7 + 8192, D, part=1, p0=0)
        onesr = VB(o7 + 10240, 512, part=1, p0=0)
        wo_v = VB(54272, NCH * D).rearrange("p (c f) -> p c f", c=NCH)
        wfT = VB(54272, NCH * D).rearrange("p (c f) -> p c f", c=NCH)
        g1r = VF(70656, D)
        be1r = VF(74752, D)
        g2r = VF(70656, D)
        be2r = VF(74752, D)
        wq_v = VB(78848, NCH * D).rearrange("p (c f) -> p c f", c=NCH)
        wk_v = VB(95232, NCH * D).rearrange("p (c f) -> p c f", c=NCH)
        wv_v = VB(111616, NCH * D).rearrange("p (c f) -> p c f", c=NCH)
        w1_v = [VB(78848 + 8192 * c, FF) for c in range(NCH)]
        w2_v = [VB(144384 + 2048 * c, D) for c in range(NFCH)]

        ps_bf = es.enter_context(nc.psum_tensor("ps_bf", [P, 1024], BF16))
        psA = es.enter_context(nc.psum_tensor("psA", [P, 1024], F32))
        psB = es.enter_context(nc.psum_tensor("psB", [P, 1024], F32))
        psC = es.enter_context(nc.psum_tensor("psC", [P, 1024], F32))
        psD = es.enter_context(nc.psum_tensor("psD", [P, 512], F32))

        sem = {}
        for name in ["qx0", "qx1", "qw", "qg_rows", "qg_wk", "qg_wv",
                     "qg_wq", "qg_wo", "qg_w2", "qg_w1", "qg_wf", "qg_dsh", "qts",
                     "qtl0", "qtl1", "qs0", "qs1", "ql0", "ql1", "ql2",
                     "qg_xc0", "qg_xc1", "qx2",
                     "pe", "dv", "ac", "po"]:
            sem[name] = es.enter_context(nc.semaphore(name))
        ct = Ctr()

        lookup = marks if marks is not None else ct.marks

        def W(eng, key):
            if key in lookup:
                name, val = lookup[key]
                eng.wait_ge(sem[name], val)

        def I(inst, name, n=1, key=None):
            inst.then_inc(sem[name], n)
            ct.inc(name, n, key=key)

        def serial(eng, name):
            class _P:
                def __getattr__(self, attr):
                    if attr in ("wait_ge",):
                        return getattr(eng, attr)
                    fn = getattr(eng, attr)

                    def call(*a, key=None, **kw):
                        c = ct.c.get(name, 0)
                        if c > 0:
                            eng.wait_ge(sem[name], c)
                        inst = fn(*a, **kw)
                        inst.then_inc(sem[name], 1)
                        ct.inc(name, 1, key=key)
                        return inst
                    return call
            return _P()

        DMA_SEMS = ["qx0", "qx1", "qw", "qg_rows", "qg_wk", "qg_wv", "qg_wq",
                    "qg_wo", "qg_w2", "qg_w1", "qg_wf", "qg_dsh", "qts",
                    "qtl0", "qtl1", "qs0", "qs1", "ql0", "ql1", "ql2",
                    "qg_xc0", "qg_xc1", "qx2"]

        def drain_tail():
            with nc.Block("drain") as dblk:
                @dblk.sync
                def _(sy):
                    for name in DMA_SEMS:
                        if ct.c.get(name, 0) > 0:
                            sy.wait_ge(sem[name], ct.c[name])

        with nc.Block("init", no_gpsimd_drain=True) as blk:
            @blk.sync
            def _(sy):
                for dst, src in [(identb, idb_d), (identf, idf_d),
                                 (iplo, iplo_d), (iphi, iphi_d), (istk, istk_d),
                                 (onec, onec_d), (onesr, ones_d), (b1c, b1_d),
                                 (epszr, epsz_d),
                                 (g1r, g1_d), (be1r, be1_d)]:
                    sy.dma_start(dst[:, :], src).then_inc(sem["qw"], 16)
                    ct.inc("qw", 16)
                ct.marks["w_consts"] = ("qw", ct.c["qw"])

            @blk.gpsimd
            def _(gp):
                for dst, src in [(bqr, bq_d), (bkr, bk_d), (bvr, bv_d),
                                 (bor, bo_d), (b2r, b2_d)]:
                    gp.dma_start(dst[:, :], src).then_inc(sem["qg_rows"], 16)
                    ct.inc("qg_rows", 16)
                ct.marks["w_rows"] = ("qg_rows", ct.c["qg_rows"])
                for w in range(NCH):
                    gp.dma_start(wk_v[:, w, :], wk_d[w * P:(w + 1) * P, :]
                                 ).then_inc(sem["qg_wk"], 16)
                    ct.inc("qg_wk", 16, key=f"w_wk{w}")
                for w in range(NCH):
                    gp.dma_start(wv_v[:, w, :], wv_d[w * P:(w + 1) * P, :]
                                 ).then_inc(sem["qg_wv"], 16)
                    ct.inc("qg_wv", 16, key=f"w_wv{w}")
                for w in range(NCH):
                    gp.dma_start(wq_v[:, w, :], wq_d[w * P:(w + 1) * P, :]
                                 ).then_inc(sem["qg_wq"], 16)
                    ct.inc("qg_wq", 16, key=f"w_wq{w}")

            @blk.vector
            def _(v):
                v = serial(v, "dv")
                v.memset(ec[:, 0:1], EPS_LN)
                v.memset(km1[:, 0:1], -1.0)
                v.memset(kmh[:, 0:1], -0.5)
                v.memset(ksz[:, :, :], 0.0, key="zmem")

        if phases < 1:
            drain_tail()
            return nc, ct
        with nc.Block("passA", no_gpsimd_drain=True) as blk:
            xbufA = [xbuf[0], xbuf[1], xb3]

            def psK(t):
                return psA if t % 2 == 0 else psB

            def psV(t):
                return psB if t % 2 == 0 else psA

            @blk.sync
            def _(sy):
                for t in range(T):
                    W(sy, f"aV3_{t - 3}")
                    qn = ["qx0", "qx1", "qx2"][t % 3]
                    sy.dma_start(xbufA[t % 3][:, :],
                                 x_d[t * P:(t + 1) * P, :]
                                 ).then_inc(sem[qn], 16)
                    ct.inc(qn, 16, key=f"ax_{t}")

            @blk.gpsimd
            def _(gp):
                for st in range(NSUP):
                    if st < NCH:
                        gp.dma_start(wo_v[:, st, :], wo_d[st * P:(st + 1) * P, :]
                                     ).then_inc(sem["qg_wo"], 16)
                        ct.inc("qg_wo", 16, key=f"w_wo{st}")
                    for c in (2 * st, 2 * st + 1):
                        gp.dma_start(w2_v[c][:, :], w2_d[c * P:(c + 1) * P, :]
                                     ).then_inc(sem["qg_w2"], 16)
                        ct.inc("qg_w2", 16, key=f"w_w2{c}")

            @blk.scalar
            def _(sc):
                raw = sc
                sc = serial(sc, "ac")

                def zst(t):
                    if t >= T:
                        return
                    W(sc, f"ax_{t}")
                    W(sc, f"zV2_{t - 2}")
                    p_ = t % 2
                    sc.activation(tf32[:, :], xbufA[t % 3][:, :], AF.Copy,
                                  accum_out=cc[:, p_:p_ + 1],
                                  key=f"zA1_{t}")

                def za2(t):
                    if t >= T:
                        return
                    p_ = t % 2
                    W(sc, f"zV2_{t}")
                    sc.activation(r0c[:, t:t + 1], cc[:, 2 + p_:3 + p_],
                                  AF.Ln, bias=EPS_LN)
                    sc.activation(r0c[:, t:t + 1], r0c[:, t:t + 1], AF.Exp,
                                  scale=-0.5, key=f"zA2_{t}")

                zst(0)
                zst(1)
                za2(0)
                za2(1)
                for st in range(NSUP):
                    for sub in range(2):
                        t = 2 * st + sub
                        if t >= 1:
                            za2(t + 1)
                        zst(t + 2)
                        W(sc, f"aP2_{t}")
                        W(sc, f"aV5_{t - 1}")
                        sc.activation(scr2[:, :], psK(t)[:, :], AF.Relu,
                                      scale=-1.0)
                        sc.activation(scr2[:, :], scr2[:, :], AF.Exp,
                                      scale=-1.0, key=f"aA3_{t}")
                        W(sc, f"aP3_{t}")
                        W(sc, f"aP4_{t - 1}")
                        sc.activation(vbf[:, :], psV(t)[:, :], AF.Copy,
                                      key=f"aA4_{t}")
                    for g in range(4):
                        half = scr1[:, (g % 2) * 512:(g % 2) * 512 + 512]
                        pq = psA[:, (g % 2) * 512:(g % 2) * 512 + 512]
                        W(sc, f"aPQg_{st}_{g}")
                        if g >= 2:
                            W(sc, f"aVQg_{st}_{g - 2}")
                        else:
                            W(sc, f"aVQg_{st - 1}_{g + 2}")
                        sc.activation(half, pq, AF.Relu, scale=-1.0)
                        sc.activation(half, half, AF.Exp, scale=-1.0,
                                      key=f"aAQg_{st}_{g}")
                    W(raw, f"aVQg_{st}_3")
                    for sub in range(2):
                        raw.dma_start(
                            qt_d[2 * st + sub],
                            qts4[:, sub, :, :].rearrange("p c s -> p (c s)")
                        ).then_inc(sem["qts"], 16)
                        ct.inc("qts", 16, key=f"aqts_{st}_{sub}")

            @blk.vector
            def _(v):
                v = serial(v, "dv")
                srcbfP = [srcbf, srcbf2]

                def emit_v3(t):
                    if t >= T:
                        return
                    W(v, f"ax_{t}")
                    W(v, f"aP1_{t - 2}")
                    W(v, f"zA2_{t}")
                    v.tensor_scalar(srcbfP[t % 2][:, :], xbufA[t % 3][:, :],
                                    m0c[:, t:t + 1], r0c[:, t:t + 1],
                                    ALU.subtract, ALU.mult,
                                    key=f"aV3_{t}")

                def zsq(t):
                    if t >= T:
                        return
                    p_ = t % 2
                    W(v, f"ax_{t}")
                    v.scalar_tensor_tensor(zdum[:, :], xbufA[t % 3][:, :],
                                           1.0, xbufA[t % 3][:, :],
                                           ALU.mult, ALU.mult,
                                           accum_out=cc[:, 2 + p_:3 + p_],
                                           key=f"zSq_{t}")

                def zcols(t):
                    if t >= T:
                        return
                    p_ = t % 2
                    W(v, f"zA1_{t}")
                    v.tensor_scalar(m0c[:, t:t + 1], cc[:, p_:p_ + 1],
                                    INV_D, None, ALU.mult)
                    v.tensor_tensor(cc[:, p_:p_ + 1], m0c[:, t:t + 1],
                                    m0c[:, t:t + 1], ALU.mult)
                    v.tensor_scalar(cc[:, 2 + p_:3 + p_],
                                    cc[:, 2 + p_:3 + p_], INV_D,
                                    cc[:, p_:p_ + 1], ALU.mult,
                                    ALU.subtract, key=f"zV2_{t}")

                def emit_v4(t):
                    sub = t % 2
                    W(v, f"aP1_{t}")
                    v.tensor_copy(srcTs[:, 0:4, sub * P:(sub + 1) * P],
                                  ps_bf[:, 0:512].rearrange(
                                      "p (c s) -> p c s", c=4),
                                  key=f"aV4a_{t}")
                    v.tensor_copy(srcTs[:, 4:8, sub * P:(sub + 1) * P],
                                  ps_bf[:, 512:1024].rearrange(
                                      "p (c s) -> p c s", c=4),
                                  key=f"aV4_{t}")

                def emit_v5(t):
                    if t < 0:
                        return
                    W(v, f"aA3_{t}")
                    W(v, f"aP4_{t - 1}")
                    v.scalar_tensor_tensor(kbf[:, :], psK(t)[:, :], 0.0,
                                           scr2[:, :], ALU.max, ALU.add,
                                           key=f"aV5_{t}")

                zsq(0)
                zcols(0)
                zsq(1)
                zcols(1)
                emit_v3(0)
                emit_v3(1)
                for st in range(NSUP):
                    for sub in range(2):
                        t = 2 * st + sub
                        emit_v4(t)
                        emit_v5(t - 1)
                        if t >= 1:
                            emit_v3(t + 1)
                        zsq(t + 2)
                        zcols(t + 2)
                    for g in range(4):
                        half = scr1[:, (g % 2) * 512:(g % 2) * 512 + 512]
                        pq = psA[:, (g % 2) * 512:(g % 2) * 512 + 512]
                        W(v, f"aAQg_{st}_{g}")
                        if g == 0:
                            W(v, f"aqts_{st - 1}_1")
                        for u_ in range(2):
                            v.scalar_tensor_tensor(
                                qts4[:, u_, 2 * g:2 * g + 2, :],
                                pq.rearrange("p (c u s) -> p c u s",
                                             c=2, u=2)[:, :, u_, :],
                                0.0,
                                half.rearrange("p (c u s) -> p c u s",
                                               c=2, u=2)[:, :, u_, :],
                                ALU.max, ALU.add,
                                key=(f"aVQg_{st}_{g}" if u_ == 1 else None))
                emit_v5(T - 1)

            @blk.tensor
            def _(te):
                W(te, "w_consts")

                def emit_p4(t):
                    first = (t == 0)
                    last = (t == T - 1)
                    W(te, f"aV5_{t}")
                    W(te, f"aA4_{t}")
                    for h in range(H):
                        col = (h % 2) * 512 + (h // 2) * DH
                        te.matmul(psC[0:DH, col:col + DH],
                                  kbf[:, h * DH:(h + 1) * DH],
                                  vbf[:, h * DH:(h + 1) * DH],
                                  start=(first and h < 2),
                                  stop=(last and h >= H - 2))
                    for h in range(H):
                        colz = (h % 2) * 8 + (h // 2)
                        ii = te.matmul(psD[0:DH, colz:colz + 1],
                                       kbf[:, h * DH:(h + 1) * DH],
                                       onec[:, 0:1],
                                       start=(first and h == 0),
                                       stop=(last and h == H - 1))
                    I(ii, "pe", key=f"aP4_{t}")

                for st in range(NSUP):
                    for sub in range(2):
                        t = 2 * st + sub
                        W(te, f"aV3_{t}")
                        W(te, f"aV4_{t - 1}")
                        sbp = srcbf if t % 2 == 0 else srcbf2
                        for c in range(NCH):
                            ii = te.transpose(ps_bf[:, c * P:(c + 1) * P],
                                              sbp[:, c * P:(c + 1) * P],
                                              identb[:, :])
                        I(ii, "pe", key=f"aP1_{t}")
                        W(te, f"aV4a_{t}")
                        W(te, f"w_wk{NCH - 1}")
                        W(te, "w_rows")
                        if sub == 0:
                            W(te, f"aVQg_{st - 1}_3")
                        else:
                            W(te, f"aA4_{t - 1}")
                        for w in range(NCH):
                            if w == 4:
                                W(te, f"aV4_{t}")
                            for nh in range(2):
                                te.matmul(psK(t)[:, nh * 512:(nh + 1) * 512],
                                          srcTs[:, w, sub * P:(sub + 1) * P],
                                          wk_v[:, w, nh * 512:(nh + 1) * 512],
                                          start=(w == 0), stop=False)
                        for nh in range(2):
                            ii = te.matmul(psK(t)[:, nh * 512:(nh + 1) * 512],
                                           onesr[:, 0:P],
                                           bkr[:, nh * 512:(nh + 1) * 512],
                                           start=False, stop=True)
                        I(ii, "pe", key=f"aP2_{t}")
                        W(te, f"w_wv{NCH - 1}")
                        if sub == 0:
                            W(te, f"aVQg_{st - 1}_3")
                            W(te, f"aV5_{t - 1}")
                        else:
                            W(te, f"aV5_{t - 1}")
                        for w in range(NCH):
                            for nh in range(2):
                                te.matmul(psV(t)[:, nh * 512:(nh + 1) * 512],
                                          srcTs[:, w, sub * P:(sub + 1) * P],
                                          wv_v[:, w, nh * 512:(nh + 1) * 512],
                                          start=(w == 0), stop=False)
                        for nh in range(2):
                            ii = te.matmul(psV(t)[:, nh * 512:(nh + 1) * 512],
                                           onesr[:, 0:P],
                                           bvr[:, nh * 512:(nh + 1) * 512],
                                           start=False, stop=True)
                        I(ii, "pe", key=f"aP3_{t}")
                        if t >= 1:
                            emit_p4(t - 1)
                    W(te, f"w_wq{NCH - 1}")
                    for g in range(4):
                        pqb = psA[:, (g % 2) * 512:(g % 2) * 512 + 512]
                        if g >= 2:
                            W(te, f"aVQg_{st}_{g - 2}")
                        elif g == 0:
                            W(te, f"aV5_{2 * st}")
                            W(te, f"aA4_{2 * st + 1}")
                        for cc_ in range(2):
                            c = 2 * g + cc_
                            pq = pqb[:, cc_ * 256:cc_ * 256 + 256]
                            te.matmul(pq, bqr[:, c * P:(c + 1) * P],
                                      onesr[:, 0:256],
                                      start=(cc_ == 0), stop=False)
                            for w in range(NCH):
                                ii = te.matmul(pq,
                                               wq_v[:, w, c * P:(c + 1) * P],
                                               srcTs[:, w, :], start=False,
                                               stop=(w == NCH - 1
                                                     and cc_ == 1))
                            if cc_ == 1:
                                I(ii, "pe", key=f"aPQg_{st}_{g}")
                emit_p4(T - 1)

        if phases < 2:
            drain_tail()
            return nc, ct
        with nc.Block("finalize", no_gpsimd_drain=True) as blk:
            @blk.vector
            def _(v):
                v = serial(v, "dv")
                v.tensor_copy(kvsb[:, :], psC[0:DH, :], key="fkv")
                v.tensor_copy(ksb[:, :], psD[0:DH, 0:H], key="fks")
                W(v, "fpad")
                v.tensor_copy(kvz[:, :], psA[:, :], key="fkvz")
                for h in range(H):
                    colz = (h % 2) * 8 + (h // 2)
                    v.tensor_copy(ksz[:, h // 2, h:h + 1],
                                  psB[:, colz:colz + 1],
                                  key=("fksz" if h == H - 1 else None))

            @blk.tensor
            def _(te):
                W(te, "fks")
                for h in range(H):
                    col = (h % 2) * 512 + (h // 2) * DH
                    pad = iplo if h % 2 == 0 else iphi
                    te.matmul(psA[:, h * DH:(h + 1) * DH], pad[:, :],
                              kvsb[:, col:col + DH],
                              start=(h in (0, 8)), stop=(h in (7, 15)))
                te.matmul(psB[:, 0:8], iplo[:, :], ksb[:, 0:8], start=True,
                          stop=False)
                ii = te.matmul(psB[:, 8:16], iphi[:, :], ksb[:, 8:16],
                               start=False, stop=True)
                I(ii, "pe", key="fpad")

        if phases < 3:
            drain_tail()
            return nc, ct
        with nc.Block("passB1", no_gpsimd_drain=True) as blk:
            srcb1 = [scr1, scr2]
            zcolsP = [zcols, zcols2]

            def psO(t):
                return psA

            def psZ(t):
                return psD[:, 0:H] if t % 2 == 0 else psD[:, H:2 * H]

            @blk.sync
            def _(sy):
                for t in range(T):
                    W(sy, f"bPo_{t - 2}")
                    sy.dma_start(qtbuf[t % 2][:, :, :],
                                 qt_d[t].rearrange("p (c s) -> p c s", c=NCH)
                                 ).then_inc(sem[f"qtl{t % 2}"], 16)
                    ct.inc(f"qtl{t % 2}", 16, key=f"bqtl_{t}")
                    W(sy, f"bVsrc_{t - 2}")
                    sy.dma_start(xbuf[t % 2][:, :], x_d[t * P:(t + 1) * P, :]
                                 ).then_inc(sem[f"qx{t % 2}"], 16)
                    ct.inc(f"qx{t % 2}", 16, key=f"bx_{t}")


            @blk.gpsimd
            def _(gp):
                def ptt(u):
                    W(gp, f"bVx1v_{u}")
                    ii = gp.tensor_tensor(stf32[u % 2][:, :],
                                          stf32[u % 2][:, :], be1r[:, :],
                                          ALU.add)
                    I(ii, "po", key=f"bVx1_{u}")
                    ct.marks[f"bG1_{u}"] = ct.marks[f"bVx1_{u}"]

                for t in range(T):
                    if t >= 2:
                        ptt(t - 2)
                    if t % 3 == 0 and t // 3 < NCH:
                        w = t // 3
                        W(gp, f"bVt_{min(3 * w, T - 1)}")
                        gp.dma_start(w1_v[w][:, :],
                                     w1_d[w * P:(w + 1) * P, :]
                                     ).then_inc(sem["qg_w1"], 16)
                        ct.inc("qg_w1", 16, key=f"w_w1{w}")
                ptt(T - 2)
                ptt(T - 1)
                W(gp, "bst1_0")
                W(gp, f"bVtq_{T - 1}")
                gp.dma_start(x1bf[:, :], x1_d[0]).then_inc(sem["qg_xc0"], 16)
                ct.inc("qg_xc0", 16, key="cVcst_0")
                W(gp, "bst1_1")
                gp.dma_start(x1bfB[:, :], x1_d[1]).then_inc(sem["qg_xc1"], 16)
                ct.inc("qg_xc1", 16, key="cVcst_1")
                W(gp, f"bVt_{T - 1}")
                gp.dma_start(wfstage[:, :], wf_d[0:P, :]
                             ).then_inc(sem["qg_wf"], 16)
                ct.inc("qg_wf", 16, key="w_wf0")

            @blk.tensor
            def _(te):
                def stage_head(t):
                    zps = psZ(t)
                    W(te, f"bqtl_{t}")
                    W(te, f"bVz2_{t - 2}")
                    if t == 0:
                        W(te, "fksz")
                    if t == 1:
                        W(te, "fkv")
                    for c in range(NCH):
                        te.matmul(zps, qtbuf[t % 2][:, c, :], ksz[:, c, :],
                                  start=(c == 0), stop=False)
                    ii = te.matmul(zps, onesr[:, 0:P], epszr[:, :],
                                   start=False, stop=True)
                    I(ii, "pe", key=f"bPZ_{t}")
                    W(te, f"bAs_{t - 1}")
                    if t == 0:
                        W(te, "fkvz")
                    for h in range(H):
                        ii = te.matmul(psO(t)[:, h * DH:(h + 1) * DH],
                                       qtbuf[t % 2][:, h // 2, :],
                                       kvz[:, h * DH:(h + 1) * DH],
                                       start=(h in (0, 8)), stop=(h in (7, 15)))
                    I(ii, "pe", key=f"bPo_{t}")

                def pe_PoT(u):
                    W(te, f"bAs_{u}")
                    W(te, f"bVoT_{u - 1}")
                    for c in range(NCH):
                        ii = te.transpose(ps_bf[:, c * P:(c + 1) * P],
                                          obf[:, c * P:(c + 1) * P],
                                          identb[:, :])
                    I(ii, "pe", key=f"bPoT_{u}")

                def pe_Pat(u):
                    pat = psB if u % 2 == 0 else psC
                    W(te, f"bVoT_{u}")
                    W(te, f"bVt_{u - 2}")
                    if u == 1:
                        W(te, "fkv")
                    W(te, f"w_wo{NCH - 1}")
                    for w in range(NCH):
                        for nh in range(2):
                            te.matmul(pat[:, nh * 512:(nh + 1) * 512],
                                      oTv[:, w * P:(w + 1) * P],
                                      wo_v[:, w, nh * 512:(nh + 1) * 512],
                                      start=(w == 0), stop=False)
                    for nh in range(2):
                        ii = te.matmul(pat[:, nh * 512:(nh + 1) * 512],
                                       onesr[:, 0:P],
                                       bor[:, nh * 512:(nh + 1) * 512],
                                       start=False, stop=True)
                    I(ii, "pe", key=f"bPat_{u}")

                for t in range(T):
                    if t >= 1:
                        pe_PoT(t - 1)
                    stage_head(t)
                    if t >= 1:
                        pe_Pat(t - 1)
                pe_PoT(T - 1)
                pe_Pat(T - 1)

            @blk.vector
            def _(v):
                v = serial(v, "dv")
                bcol = [(4, 5, 6, 7, 8, 9), (10, 11, 12, 13, 14, 15)]

                def vec_head(t):
                    W(v, f"bx_{t}")
                    W(v, f"bVt_{t - 2}")
                    v.tensor_scalar(srcb1[t % 2][:, :], xbuf[t % 2][:, :],
                                    m0c[:, t:t + 1], r0c[:, t:t + 1],
                                    ALU.subtract, ALU.mult, key=f"bVsrc_{t}")
                    W(v, f"bPZ_{t}")
                    W(v, f"bAs_{t - 2}")
                    v.reciprocal(zcolsP[t % 2][:, :], psZ(t),
                                 key=f"bVz2_{t}")
                    W(v, f"bPo_{t}")
                    W(v, f"bPoT_{t - 1}")
                    W(v, f"bVtq_{t - 2}")
                    zc = zcolsP[t % 2]
                    zb = bass.AP(zc.tensor, zc.offset,
                                 [list(zc.ap[0]), [1, H], [0, DH]])
                    v.tensor_tensor(
                        obf[:, :].rearrange("p (h j) -> p h j", h=H),
                        psO(t)[:, :].rearrange("p (h j) -> p h j", h=H),
                        zb, ALU.mult, key=f"bAs_{t}")

                def vec_t(u):
                    cs = bcol[u % 2]
                    pat = psB if u % 2 == 0 else psC
                    W(v, f"bPat_{u}")
                    W(v, f"bVtq_{u - 1}")
                    v.scalar_tensor_tensor(tf32[:, :], pat[:, :], 1.0,
                                           srcb1[u % 2][:, :], ALU.mult,
                                           ALU.add,
                                           accum_out=cc[:, cs[0]:cs[0] + 1],
                                           key=f"bVt_{u}")

                def vec_x1(u):
                    cs = bcol[u % 2]
                    W(v, f"bVtq_{u}")
                    v.tensor_scalar(cc[:, cs[2]:cs[2] + 1],
                                    cc[:, cs[0]:cs[0] + 1], INV_D, None,
                                    ALU.mult)
                    v.tensor_tensor(cc[:, cs[3]:cs[3] + 1],
                                    cc[:, cs[2]:cs[2] + 1],
                                    cc[:, cs[2]:cs[2] + 1], ALU.mult)
                    v.tensor_scalar(cc[:, cs[4]:cs[4] + 1],
                                    cc[:, cs[1]:cs[1] + 1], INV_D,
                                    cc[:, cs[3]:cs[3] + 1], ALU.mult,
                                    ALU.subtract, key=f"bVc_{u}")
                    W(v, f"bAr1_{u}")
                    if u < 2:
                        W(v, f"aqts_{NSUP - 1}_1")
                    W(v, f"bG1_{u - 2}")
                    W(v, f"bst1_{u - 2}")
                    v.scalar_tensor_tensor(stf32[u % 2][:, :], tf32[:, :],
                                           cc[:, cs[2]:cs[2] + 1], g1r[:, :],
                                           ALU.subtract, ALU.mult)
                    v.tensor_scalar(stf32[u % 2][:, :], stf32[u % 2][:, :],
                                    cc[:, cs[5]:cs[5] + 1], None, ALU.mult,
                                    key=f"bVx1v_{u}")

                for t in range(T):
                    vec_head(t)
                    if t >= 2:
                        vec_x1(t - 2)
                    if t >= 1:
                        vec_t(t - 1)
                vec_x1(T - 2)
                vec_t(T - 1)
                vec_x1(T - 1)

            @blk.scalar
            def _(sc):
                raw = sc
                sc = serial(sc, "ac")
                bcol = [(4, 5, 6, 7, 8, 9), (10, 11, 12, 13, 14, 15)]

                def sc_VoT(u):
                    W(sc, f"bPoT_{u}")
                    sc.copy(oTv[:, :], ps_bf[:, :], key=f"bVoT_{u}")

                def sc_Sq(u):
                    cs = bcol[u % 2]
                    W(sc, f"bVt_{u}")
                    sc.activation(obf[:, :], tf32[:, :], AF.Square,
                                  accum_out=cc[:, cs[1]:cs[1] + 1],
                                  key=f"bVtq_{u}")
                    W(sc, f"bVc_{u}")
                    sc.activation(cc[:, cs[5]:cs[5] + 1],
                                  cc[:, cs[4]:cs[4] + 1], AF.Ln,
                                  bias=EPS_LN)
                    sc.activation(cc[:, cs[5]:cs[5] + 1],
                                  cc[:, cs[5]:cs[5] + 1], AF.Exp,
                                  scale=-0.5, key=f"bAr1_{u}")

                def sc_store(u):
                    W(raw, f"bG1_{u}")
                    raw.dma_start(x1_d[u], stf32[u % 2][:, :]
                                  ).then_inc(sem[f"qs{u % 2}"], 16)
                    ct.inc(f"qs{u % 2}", 16, key=f"bst1_{u}")

                for t in range(T):
                    if t >= 1:
                        sc_VoT(t - 1)
                    if t >= 2:
                        sc_Sq(t - 2)
                    if t >= 3:
                        sc_store(t - 3)
                sc_VoT(T - 1)
                sc_Sq(T - 2)
                sc_Sq(T - 1)
                for u in (T - 3, T - 2, T - 1):
                    sc_store(u)

        if phases < 4:
            drain_tail()
            return nc, ct
        with nc.Block("passB2a", no_gpsimd_drain=True) as blk:
            @blk.sync
            def _(sy):
                sy.dma_start(g2r[:, :], g2_d).then_inc(sem["qw"], 16)
                ct.inc("qw", 16)
                sy.dma_start(be2r[:, :], be2_d).then_inc(sem["qw"], 16)
                ct.inc("qw", 16, key="w_g2")
                for t in range(T):
                    W(sy, f"cVuP_{t - 3}")
                    W(sy, f"bst1_{t}")
                    sy.dma_start(ldf3[t % 3][:, :], x1_d[t]
                                 ).then_inc(sem[f"ql{t % 3}"], 16)
                    ct.inc(f"ql{t % 3}", 16, key=f"clx1_{t}")
                W(sy, f"cVuP_{T - 2}")
                W(sy, "cstu_0")
                sy.dma_start(ldf32[0][:, :], u_d[0]).then_inc(sem["ql0"], 16)
                ct.inc("ql0", 16, key="dlu_0")
                W(sy, f"cVuP_{T - 1}")
                W(sy, "cstu_1")
                sy.dma_start(ldf32[1][:, :], u_d[1]).then_inc(sem["ql1"], 16)
                ct.inc("ql1", 16, key="dlu_1")


            @blk.gpsimd
            def _(gp):
                def xc(t):
                    if t >= T:
                        return
                    W(gp, f"bst1_{t}")
                    W(gp, f"cPt_{t - 2}")
                    xb = x1bf if t % 2 == 0 else x1bfB
                    gp.dma_start(xb[:, :], x1_d[t]
                                 ).then_inc(sem[f"qg_xc{t % 2}"], 16)
                    ct.inc(f"qg_xc{t % 2}", 16, key=f"cVcst_{t}")

                gp.dma_start(dshb[:, :], dsh_d).then_inc(sem["qg_dsh"], 16)
                ct.inc("qg_dsh", 16, key="w_dsh")
                for st in range(NSUP):
                    if 1 <= st < NCH:
                        W(gp, f"cVW_{st - 1}")
                        gp.dma_start(wfstage[:, :],
                                     wf_d[st * P:(st + 1) * P, :]
                                     ).then_inc(sem["qg_wf"], 16)
                        ct.inc("qg_wf", 16, key=f"w_wf{st}")
                    xc(2 * st + 2)
                    xc(2 * st + 3)

            @blk.tensor
            def _(te):
                psDbf = psD.ap().bitcast(BF16)
                for st in range(NSUP):
                    for sub in range(2):
                        t = 2 * st + sub
                        pt = ps_bf[:, :] if sub == 0 else psDbf[:, 0:1024]
                        xb = x1bf if t % 2 == 0 else x1bfB
                        W(te, f"cVcst_{t}")
                        W(te, f"cVxT_{t - 2}")
                        if sub == 0:
                            W(te, f"cVW_{st - 1}")
                        for c in range(NCH):
                            ii = te.transpose(pt[:, c * P:(c + 1) * P],
                                              xb[:, c * P:(c + 1) * P],
                                              identb[:, :])
                        I(ii, "pe", key=f"cPt_{t}")
                    if st < NCH:
                        W(te, f"w_wf{st}")
                        W(te, f"cVxT_{2 * st}")
                        for c in range(NCH):
                            ii = te.transpose(ps_bf[:, c * P:(c + 1) * P],
                                              wfstage[:, c * P:(c + 1) * P],
                                              identb[:, :])
                        I(ii, "pe", key=f"cPW_{st}")
                    W(te, f"cVxT_{2 * st + 1}")
                    if st == 0:
                        W(te, f"w_w1{NCH - 1}")
                        W(te, f"w_w2{NFCH - 1}")
                    x1t = x1Ts if st % 2 == 0 else x1Ts2

                    def emit_ph(c):
                        ph = psC[:, 0:256] if c % 2 == 0 else psC[:, 512:768]
                        if c >= 2:
                            W(te, f"cAg_{st}_{c - 2}")
                        else:
                            W(te, f"cAg_{st - 1}_{NFCH - 2 + c}")
                        for w in range(NCH):
                            ii = te.matmul(ph, w1_v[w][:, c * P:(c + 1) * P],
                                           x1t[:, w, :], start=(w == 0),
                                           stop=(w == NCH - 1))
                        I(ii, "pe", key=f"cPh_{st}_{c}")

                    def emit_py(c):
                        W(te, f"cAg_{st}_{c}")
                        for sub in range(2):
                            psY = psA if sub == 0 else psB
                            if c == 0:
                                W(te, f"cVuP_{2 * (st - 1) + sub}")
                            for nh in range(2):
                                ii = te.matmul(
                                    psY[:, nh * 512:(nh + 1) * 512],
                                    hT[c % 2][:, sub * P:(sub + 1) * P],
                                    w2_v[c][:, nh * 512:(nh + 1) * 512],
                                    start=(c == 0), stop=False)
                        I(ii, "pe", key=f"cPy_{st}_{c}")

                    for c in range(NFCH):
                        emit_ph(c)
                        if c >= 1:
                            emit_py(c - 1)
                    emit_py(NFCH - 1)
                    for sub in range(2):
                        psY = psA if sub == 0 else psB
                        for nh in range(2):
                            ii = te.matmul(psY[:, nh * 512:(nh + 1) * 512],
                                           onesr[:, 0:P],
                                           b2r[:, nh * 512:(nh + 1) * 512],
                                           start=False, stop=True)
                        I(ii, "pe", key=f"cPyb_{st}_{sub}")

            @blk.vector
            def _(v):
                v = serial(v, "dv")

                psDbf_v = psD.ap().bitcast(BF16)

                def cvxt(t):
                    if t >= T:
                        return
                    st = t // 2
                    sub = t % 2
                    pt = ps_bf[:, :] if sub == 0 else psDbf_v[:, 0:1024]
                    W(v, f"cPt_{t}")
                    W(v, f"cPh_{st - 2}_{NFCH - 1}")
                    x1tv = x1Ts if st % 2 == 0 else x1Ts2
                    v.tensor_copy(x1tv[:, :, sub * P:(sub + 1) * P],
                                  pt.rearrange("p (c s) -> p c s", c=NCH),
                                  key=f"cVxT_{t}")

                def cvw(st):
                    pass

                def cvu(t):
                    st = t // 2
                    sub = t % 2
                    psY = psA if sub == 0 else psB
                    W(v, f"cPyb_{st}_{sub}")
                    W(v, f"clx1_{t}")
                    if t < 2:
                        W(v, f"bst1_{T - 2 + t}")
                    W(v, f"cstu_{t - 2}")
                    W(v, f"cSq_{t - 2}")
                    v.scalar_tensor_tensor(stf32[t % 2][:, :], psY[:, :],
                                           1.0, ldf3[t % 3][:, :],
                                           ALU.mult, ALU.add,
                                           accum_out=usc[:, t:t + 1],
                                           key=f"cVuP_{t}")

                for st in range(NSUP):
                    if st >= 1:
                        cvu(2 * st - 2)
                    cvxt(2 * st)
                    cvxt(2 * st + 1)
                    if st >= 1:
                        cvu(2 * st - 1)
                    cvw(st)
                cvu(T - 2)
                cvu(T - 1)

            @blk.scalar
            def _(sc):
                raw = sc
                sc = serial(sc, "ac")


                def scw(st):
                    if st < NCH:
                        W(sc, f"cPW_{st}")
                        sc.copy(wfT[:, :, st * P:(st + 1) * P],
                                ps_bf[:, :].rearrange("p (c s) -> p c s",
                                                      c=NCH),
                                key=f"cVW_{st}")

                for st in range(NSUP):
                    scw(st)
                    for c in range(NFCH):
                        ph = psC[:, 0:256] if c % 2 == 0 else psC[:, 512:768]
                        W(sc, f"cPh_{st}_{c}")
                        if c >= 2:
                            W(sc, f"cPy_{st}_{c - 2}")
                        else:
                            W(sc, f"cPy_{st - 1}_{NFCH - 2 + c}")
                        sc.activation(hT[c % 2][:, :], ph, AF.Gelu,
                                        bias=b1c[:, c:c + 1], key=f"cAg_{st}_{c}")
                    for sub in range(2):
                        t = 2 * st + sub
                        W(sc, f"cVuP_{t}")
                        sc.activation(x1bf2[:, :], stf32[t % 2][:, :],
                                      AF.Square, accum_out=uqc[:, t:t + 1],
                                      key=f"cSq_{t}")
                        W(raw, f"cSq_{t}")
                        raw.dma_start(u_d[t], stf32[t % 2][:, :]
                                      ).then_inc(sem[f"qs{t % 2}"], 16)
                        ct.inc(f"qs{t % 2}", 16, key=f"cstu_{t}")

        if phases < 5:
            drain_tail()
            return nc, ct
        with nc.Block("passB2b", no_gpsimd_drain=True) as blk:
            psBbf = psB.ap().bitcast(BF16)
            scrAP = [scrA, scrA2]
            xAvP = [xAv, xAv2]
            nbfP = [nbf, nbf2]
            nTP = [nTv, nT2]
            poolTP = [poolT, poolT2]
            xAv4P = [xAv, xAv2, xAv3, xAv4]

            def i4(t):
                return ((t // 2) % 2) * 2 + (t % 2)
            colset = [(4, 5, 14, 15, 7, 12), (0, 1, 2, 3, 10, 11)]

            @blk.sync
            def _(sy):
                for t in range(T):
                    if t >= 2:
                        W(sy, f"dVg2_{t - 2}")
                        W(sy, f"cstu_{t}")
                        sy.dma_start(ldf32[t % 2][:, :], u_d[t]
                                     ).then_inc(sem[f"ql{t % 2}"], 16)
                        ct.inc(f"ql{t % 2}", 16, key=f"dlu_{t}")
                    W(sy, f"dVxA_{t - 2}")
                    sy.dma_start(xbuf[t % 2][:, :], x_d[t * P:(t + 1) * P, :]
                                 ).then_inc(sem[f"qx{t % 2}"], 16)
                    ct.inc(f"qx{t % 2}", 16, key=f"dx_{t}")

            @blk.vector
            def _(v):
                v = serial(v, "dv")
                v.tensor_scalar(m0c[:, :], usc[:, :], INV_D, None, ALU.mult)
                v.tensor_tensor(usc[:, :], m0c[:, :], m0c[:, :], ALU.mult)
                v.tensor_scalar(uqc[:, :], uqc[:, :], INV_D, None, ALU.mult)
                v.tensor_tensor(uqc[:, :], uqc[:, :], usc[:, :], ALU.subtract,
                                key="dVln2")

                def vg2(t):
                    cs = colset[t % 2]
                    if t == 0:
                        W(v, "w_g2")
                        W(v, "dAr2")
                    W(v, f"dlu_{t}")
                    v.scalar_tensor_tensor(scrAP[t % 2][:, :],
                                           ldf32[t % 2][:, :],
                                           m0c[:, t:t + 1], g2r[:, :],
                                           ALU.subtract, ALU.mult,
                                           key=f"dVg2_{t}")

                def vcols(u):
                    cs = colset[u % 2]
                    W(v, f"dSq_{u}")
                    v.tensor_scalar(cc[:, cs[2]:cs[2] + 1],
                                    cc[:, cs[0]:cs[0] + 1], INV_D, None,
                                    ALU.mult)
                    v.tensor_tensor(cc[:, cs[4]:cs[4] + 1],
                                    cc[:, cs[2]:cs[2] + 1],
                                    cc[:, cs[2]:cs[2] + 1], ALU.mult)
                    v.tensor_scalar(cc[:, cs[3]:cs[3] + 1],
                                    cc[:, cs[1]:cs[1] + 1], INV_D,
                                    cc[:, cs[4]:cs[4] + 1], ALU.mult,
                                    ALU.subtract, key=f"dVc3_{u}")

                def vxa(t):
                    cs = colset[t % 2]
                    W(v, f"dXbe_{t}")
                    v.scalar_tensor_tensor(xAv4P[i4(t)][:, :],
                                           scrAP[t % 2][:, :],
                                           r0c[:, t:t + 1],
                                           xbuf[t % 2][:, :], ALU.mult,
                                           ALU.add,
                                           accum_out=cc[:, cs[0]:cs[0] + 1],
                                           key=f"dVxA_{t}")

                def vn(u):
                    cs = colset[u % 2]
                    W(v, f"dAr3_{u}")
                    W(v, f"dPn_{u - 2}")
                    v.tensor_scalar(nbfP[u % 2][:, :], xAv4P[i4(u)][:, :],
                                    cc[:, cs[2]:cs[2] + 1],
                                    cc[:, cs[5]:cs[5] + 1], ALU.subtract,
                                    ALU.mult, key=f"dVn_{u}")

                def vout(u):
                    ps = psA if u % 2 == 0 else psC
                    W(v, f"dPr_{u}")
                    if u < 2:
                        W(v, f"cstu_{T - 2 + u}")
                    W(v, f"dsto_{u - 2}")
                    v.tensor_tensor(stf32[u % 2][:, :], ps[:, :],
                                    xAv4P[i4(u)][:, :], ALU.add,
                                    key=f"dVout_{u}")

                for p in range(T // 2):
                    t, t2 = 2 * p, 2 * p + 1
                    vg2(t)
                    vg2(t2)
                    if p >= 1:
                        vcols(t - 2)
                        vcols(t2 - 2)
                    if p >= 2:
                        vout(t - 4)
                        vout(t2 - 4)
                    if p >= 1:
                        vn(t - 2)
                        vn(t2 - 2)
                    vxa(t)
                    vxa(t2)
                vcols(T - 2)
                vcols(T - 1)
                vout(T - 4)
                vout(T - 3)
                vn(T - 2)
                vn(T - 1)
                vout(T - 2)
                vout(T - 1)

            @blk.scalar
            def _(sc):
                raw = sc
                sc = serial(sc, "ac")
                W(sc, "dVln2")
                sc.activation(r0c[:, :], uqc[:, :], AF.Ln, bias=EPS_LN)
                sc.activation(r0c[:, :], r0c[:, :], AF.Exp, scale=-0.5,
                              key="dAr2")
                def ar3(u):
                    cs = colset[u % 2]
                    W(sc, f"dVc3_{u}")
                    sc.activation(cc[:, cs[5]:cs[5] + 1],
                                  cc[:, cs[3]:cs[3] + 1], AF.Ln,
                                  bias=EPS_LN)
                    sc.activation(cc[:, cs[5]:cs[5] + 1],
                                  cc[:, cs[5]:cs[5] + 1], AF.Exp,
                                  scale=-0.5, key=f"dAr3_{u}")

                def nT_(u):
                    pn = ps_bf[:, :] if u % 2 == 0 else psBbf[:, 0:1024]
                    W(sc, f"dPr_{u - 2}")
                    W(sc, f"dPn_{u}")
                    sc.copy(nTP[u % 2][:, :, :],
                            pn.rearrange("p (c s) -> p c s", c=NCH),
                            key=f"dVnT_{u}")

                def pool_(u):
                    pp = psD[0:DH, 0:P] if u % 2 == 0 \
                        else psB[0:DH, 512:640]
                    W(sc, f"dPp_{u}")
                    sc.copy(poolTP[u % 2][:, :], pp, key=f"dVpo_{u}")

                def sq(t):
                    cs = colset[t % 2]
                    W(sc, f"dVxA_{t}")
                    sc.activation(nbfP[t % 2][:, :], xAv4P[i4(t)][:, :],
                                  AF.Square,
                                  accum_out=cc[:, cs[1]:cs[1] + 1],
                                  key=f"dSq_{t}")

                def sto(u):
                    W(raw, f"dVout_{u}")
                    raw.dma_start(out_d[u * P:(u + 1) * P, :],
                                  stf32[u % 2][:, :]
                                  ).then_inc(sem[f"qs{u % 2}"], 16)
                    ct.inc(f"qs{u % 2}", 16, key=f"dsto_{u}")

                for p in range(T // 2):
                    t, t2 = 2 * p, 2 * p + 1
                    if p >= 1:
                        ar3(t - 2)
                        ar3(t2 - 2)
                        nT_(t - 2)
                        nT_(t2 - 2)
                        pool_(t - 2)
                        pool_(t2 - 2)
                    if p >= 2:
                        sto(t - 4)
                        sto(t2 - 4)
                    sq(t)
                    sq(t2)
                ar3(T - 2)
                ar3(T - 1)
                nT_(T - 2)
                nT_(T - 1)
                pool_(T - 2)
                pool_(T - 1)
                for u in (T - 4, T - 3, T - 2, T - 1):
                    sto(u)

            @blk.gpsimd
            def _(gp):
                sgp = serial(gp, "po")
                for t in range(T):
                    W(gp, f"dx_{t}")
                    if t == 0:
                        W(gp, "w_g2")
                    sgp.tensor_tensor(xbuf[t % 2][:, :], xbuf[t % 2][:, :],
                                      be2r[:, :], ALU.add, key=f"dXbe_{t}")

            @blk.tensor
            def _(te):
                def pn_(u):
                    pn = ps_bf[:, :] if u % 2 == 0 else psBbf[:, 0:1024]
                    W(te, f"dVn_{u}")
                    W(te, f"dVnT_{u - 2}")
                    for c in range(NCH):
                        ii = te.transpose(pn[:, c * P:(c + 1) * P],
                                          nbfP[u % 2][:, c * P:(c + 1) * P],
                                          identb[:, :])
                    I(ii, "pe", key=f"dPn_{u}")

                def pp_(u):
                    pp = psD[0:DH, 0:P] if u % 2 == 0 else psB[0:DH, 512:640]
                    W(te, f"dVnT_{u}")
                    W(te, f"dVpo_{u - 2}")
                    for c in range(NCH):
                        ii = te.matmul(pp, istk[:, :], nTP[u % 2][:, c, :],
                                       start=(c == 0), stop=(c == NCH - 1))
                    I(ii, "pe", key=f"dPp_{u}")

                def pr_(u):
                    ps = psA if u % 2 == 0 else psC
                    W(te, f"dVout_{u - 2}")
                    if u == 0:
                        W(te, "w_dsh")
                        W(te, f"cVW_{NCH - 1}")
                    for w in range(NCH):
                        for nh in range(2):
                            te.matmul(ps[:, nh * 512:(nh + 1) * 512],
                                      nTP[u % 2][:, w, :],
                                      wfT[:, w, nh * 512:(nh + 1) * 512],
                                      start=(w == 0), stop=False)
                    W(te, f"dVpo_{u}")
                    for nh in range(2):
                        ii = te.matmul(ps[:, nh * 512:(nh + 1) * 512],
                                       poolTP[u % 2][:, :],
                                       dshb[:, nh * 512:(nh + 1) * 512],
                                       start=False, stop=True)
                    I(ii, "pe", key=f"dPr_{u}")

                for p in range(1, T // 2):
                    t, t2 = 2 * p, 2 * p + 1
                    pn_(t - 2)
                    pn_(t2 - 2)
                    pp_(t - 2)
                    pp_(t2 - 2)
                    pr_(t - 2)
                    pr_(t2 - 2)
                pn_(T - 2)
                pn_(T - 1)
                pp_(T - 2)
                pp_(T - 1)
                pr_(T - 2)
                pr_(T - 1)

        drain_tail()
    return nc, ct


def build_full(debug=False, phases=5):
    _, ct1 = build_nc(debug=debug, phases=phases)
    nc, _ = build_nc(debug=debug, phases=phases, marks=ct1.marks)
    return nc


_DBG = [False, 5]


def kernel(**inputs):
    x = np.asarray(inputs["L0_attn_src"], dtype=np.float32)
    B = x.shape[0]
    w = {k: np.asarray(v, dtype=np.float32) for k, v in inputs.items()}

    ident = np.eye(P, dtype=np.float32)
    shared = {
        "Wq": w["L0_attn_Wq"], "Wk": w["L0_attn_Wk"], "Wv": w["L0_attn_Wv"],
        "Wo": w["L0_attn_Wo"], "W1": w["L0_attn_W1"], "W2": w["L0_attn_W2"],
        "Wf": w["L0_ffn_weight"], "disha": w["L0_ffn_disha"],
        "bq": w["L0_attn_bq"].reshape(1, D),
        "bk": w["L0_attn_bk"].reshape(1, D),
        "bv": w["L0_attn_bv"].reshape(1, D),
        "bo": w["L0_attn_bo"].reshape(1, D),
        "b1c": np.ascontiguousarray(
            w["L0_attn_b1"].reshape(NFCH, P).T),
        "b2": w["L0_attn_b2"].reshape(1, D),
        "g1r": np.ascontiguousarray(np.broadcast_to(w["L0_attn_g1"], (P, D))),
        "be1r": np.ascontiguousarray(np.broadcast_to(w["L0_attn_be1"], (P, D))),
        "g2r": np.ascontiguousarray(np.broadcast_to(w["L0_attn_g2"], (P, D))),
        "be2r": np.ascontiguousarray(np.broadcast_to(w["L0_attn_be2"], (P, D))),
        "identb": ident.astype(BF),
        "identf": ident,
        "ipadlo": np.concatenate([np.eye(DH), np.zeros((DH, DH))],
                                 axis=1).astype(BF),
        "ipadhi": np.concatenate([np.zeros((DH, DH)), np.eye(DH)],
                                 axis=1).astype(BF),
        "istack": np.concatenate([np.eye(DH), np.eye(DH)], axis=0).astype(BF),
        "onesr": np.ones((1, 512), dtype=np.float32).astype(BF),
        "onesc": np.ones((P, 1), dtype=np.float32).astype(BF),
        "epsz": np.full((1, H), EPS_Z, dtype=np.float32).astype(BF),
    }

    nc = build_full(debug=_DBG[0], phases=_DBG[1])
    in_maps = []
    for b in range(B):
        m = dict(shared)
        m["x"] = np.ascontiguousarray(x[b])
        in_maps.append(m)
    res = run_bass_kernel_spmd(nc, in_maps, core_ids=list(range(B)))
    out = np.stack([np.asarray(r["out"]) for r in res.results], axis=0)
    return out.astype(np.float32)


# version 34
# speedup vs baseline: 1.0305x; 1.0070x over previous
import contextlib
import numpy as np
import ml_dtypes
import concourse.bass as bass
import concourse.mybir as mybir
from concourse.bass_utils import run_bass_kernel_spmd

F32 = mybir.dt.float32
BF16 = mybir.dt.bfloat16
U8 = mybir.dt.uint8
AF = mybir.ActivationFunctionType
ALU = mybir.AluOpType

S, D, P, T, H, DH, FF = 4096, 1024, 128, 32, 16, 64, 4096
NCH = D // P
NFCH = FF // P
NSUP = T // 2
INV_D = 1.0 / D
EPS_LN = 1e-5
EPS_Z = 1e-6
BF = np.dtype(ml_dtypes.bfloat16)


class Ctr:
    def __init__(self):
        self.c = {}
        self.marks = {}

    def inc(self, name, n=1, key=None):
        self.c[name] = self.c.get(name, 0) + n
        if key is not None:
            self.marks[key] = (name, self.c[name])
        return self.c[name]


def build_nc(debug=False, phases=5, marks=None):
    nc = bass.Bass("TRN2", target_bir_lowering=False)

    x_d = nc.dram_tensor("x", [S, D], F32, kind="ExternalInput").ap()
    wq_d = nc.dram_tensor("Wq", [D, D], F32, kind="ExternalInput").ap()
    wk_d = nc.dram_tensor("Wk", [D, D], F32, kind="ExternalInput").ap()
    wv_d = nc.dram_tensor("Wv", [D, D], F32, kind="ExternalInput").ap()
    wo_d = nc.dram_tensor("Wo", [D, D], F32, kind="ExternalInput").ap()
    w1_d = nc.dram_tensor("W1", [D, FF], F32, kind="ExternalInput").ap()
    w2_d = nc.dram_tensor("W2", [FF, D], F32, kind="ExternalInput").ap()
    wf_d = nc.dram_tensor("Wf", [D, D], F32, kind="ExternalInput").ap()
    dsh_d = nc.dram_tensor("disha", [DH, D], F32, kind="ExternalInput").ap()
    bq_d = nc.dram_tensor("bq", [1, D], F32, kind="ExternalInput").ap()
    bk_d = nc.dram_tensor("bk", [1, D], F32, kind="ExternalInput").ap()
    bv_d = nc.dram_tensor("bv", [1, D], F32, kind="ExternalInput").ap()
    bo_d = nc.dram_tensor("bo", [1, D], F32, kind="ExternalInput").ap()
    b1_d = nc.dram_tensor("b1c", [P, NFCH], F32, kind="ExternalInput").ap()
    b2_d = nc.dram_tensor("b2", [1, D], F32, kind="ExternalInput").ap()
    g1_d = nc.dram_tensor("g1r", [P, D], F32, kind="ExternalInput").ap()
    be1_d = nc.dram_tensor("be1r", [P, D], F32, kind="ExternalInput").ap()
    g2_d = nc.dram_tensor("g2r", [P, D], F32, kind="ExternalInput").ap()
    be2_d = nc.dram_tensor("be2r", [P, D], F32, kind="ExternalInput").ap()
    idb_d = nc.dram_tensor("identb", [P, P], BF16, kind="ExternalInput").ap()
    idf_d = nc.dram_tensor("identf", [P, P], F32, kind="ExternalInput").ap()
    iplo_d = nc.dram_tensor("ipadlo", [DH, P], BF16, kind="ExternalInput").ap()
    iphi_d = nc.dram_tensor("ipadhi", [DH, P], BF16, kind="ExternalInput").ap()
    istk_d = nc.dram_tensor("istack", [P, DH], BF16, kind="ExternalInput").ap()
    ones_d = nc.dram_tensor("onesr", [1, 512], BF16, kind="ExternalInput").ap()
    onec_d = nc.dram_tensor("onesc", [P, 1], BF16, kind="ExternalInput").ap()
    epsz_d = nc.dram_tensor("epsz", [1, H], BF16, kind="ExternalInput").ap()

    out_d = nc.dram_tensor("out", [S, D], F32, kind="ExternalOutput").ap()
    skind = "ExternalOutput" if debug else "Internal"
    qt_d = nc.dram_tensor("qt_s", [T, P, D], BF16, kind=skind).ap()
    x1_d = nc.dram_tensor("x1_s", [T, P, D], F32, kind=skind).ap()
    u_d = nc.dram_tensor("u_s", [T, P, D], F32, kind=skind).ap()

    ARENA = 212736
    es = contextlib.ExitStack()
    with es:
        arena = es.enter_context(nc.sbuf_tensor("arena", [P, ARENA], U8))
        av = arena.ap()

        def VF(off, n, part=P, p0=0):
            return av[p0:p0 + part, off:off + 4 * n].bitcast(F32)

        def VB(off, n, part=P, p0=0):
            return av[p0:p0 + part, off:off + 2 * n].bitcast(BF16)

        xbuf = [VF(0, D), VF(4096, D)]
        qtsup = VB(8192, NCH * 256).rearrange("p (c s) -> p c s", c=NCH)
        qts4 = VB(8192, NCH * 256).rearrange("p (u c s) -> p u c s", u=2, c=NCH)
        stf32 = [VF(8192, D), VF(12288, D)]
        srcTs = VB(16384, NCH * 256).rearrange("p (c s) -> p c s", c=NCH)
        qtbuf = [VB(16384, D).rearrange("p (c s) -> p c s", c=NCH),
                 VB(18432, D).rearrange("p (c s) -> p c s", c=NCH)]
        tf32 = VF(20480, D)
        ldf32 = [VF(16384, D), VF(20480, D)]
        ldf3 = [VF(16384, D), VF(20480, D), VF(4096, D)]
        scr1 = VF(24576, D)
        scr2bf = VB(28672, D)
        x1bf2 = VB(30720, D)
        x1bfB = VB(209920, D)
        scr2 = VF(28672, D)
        kvsb = VB(24576, D, part=DH)
        x1Ts = VB(24576, NCH * 256).rearrange("p (c s) -> p c s", c=NCH)
        x1Ts2 = VB(0, NCH * 256).rearrange("p (c s) -> p c s", c=NCH)
        wfstage = VB(28672, D)
        scrA = VF(24576, D)
        scrA2 = VF(78848, D)
        xAv2 = VF(82944, D)
        nbf2 = VB(87040, D)
        nT2 = VB(89088, D).rearrange("p (c s) -> p c s", c=NCH)
        poolT2 = VB(91136, P, part=DH)
        xAv3 = VF(93184, D)
        xAv4 = VF(97280, D)
        xb3 = VF(128000, D)
        zdum = VB(132096, D)
        xAv = VF(28672, D)
        srcbf = VB(32768, D)
        srcbf2 = VB(12288, D)
        kbf = VB(34816, D)
        vbf = VB(36864, D)
        obf = VB(32768, D)
        oTv = VB(34816, D)
        kvz = VB(36864, D)
        x1bf = VB(32768, D)
        hT = [VB(34816, 256), VB(35328, 256)]
        nbf = VB(32768, D)
        nTv = VB(34816, D).rearrange("p (c s) -> p c s", c=NCH)
        dshb = VB(36864, D, part=DH)
        o6 = 38912
        identb = VB(o6, P)
        identf = VF(o6 + 256, P)
        iplo = VB(o6 + 768, P, part=DH)
        iphi = VB(o6 + 1024, P, part=DH)
        istk = VB(o6 + 1280, DH)
        onec = VB(o6 + 1408, 1)
        m0c = VF(o6 + 1472, T)
        r0c = VF(o6 + 1600, T)
        usc = VF(o6 + 1728, T)
        uqc = VF(o6 + 1856, T)
        ksz = VB(o6 + 1984, NCH * H).rearrange("p (c h) -> p c h", c=NCH)
        zscrT = VF(o6 + 2240, P, part=H)
        zcols = VF(o6 + 2752, H)
        zcols2 = VF(o6 + 2928, H)
        cc = VF(o6 + 2816, 16)
        ec = VF(o6 + 2880, 1)
        km1 = VF(o6 + 2884, 1)
        kmh = VF(o6 + 2888, 1)
        ksb = VB(o6 + 2892, H, part=DH)
        poolT = VB(o6 + 3000, P, part=DH)
        epszr = VB(o6 + 3456, H, part=1)
        nc.const_aps.register(F32, EPS_LN, ec[:, 0:1])
        nc.const_aps.register(F32, -1.0, km1[:, 0:1])
        nc.const_aps.register(F32, -0.5, kmh[:, 0:1])
        b1c = VF(o6 + 3300, NFCH)
        o7 = 43008
        bqr = VB(o7, D, part=1, p0=0)
        bkr = VB(o7 + 2048, D, part=1, p0=0)
        bvr = VB(o7 + 4096, D, part=1, p0=0)
        bor = VB(o7 + 6144, D, part=1, p0=0)
        b2r = VB(o7 + 8192, D, part=1, p0=0)
        onesr = VB(o7 + 10240, 512, part=1, p0=0)
        wo_v = VB(54272, NCH * D).rearrange("p (c f) -> p c f", c=NCH)
        wfT = VB(54272, NCH * D).rearrange("p (c f) -> p c f", c=NCH)
        g1r = VF(70656, D)
        be1r = VF(74752, D)
        g2r = VF(70656, D)
        be2r = VF(74752, D)
        wq_v = VB(78848, NCH * D).rearrange("p (c f) -> p c f", c=NCH)
        wk_v = VB(95232, NCH * D).rearrange("p (c f) -> p c f", c=NCH)
        wv_v = VB(111616, NCH * D).rearrange("p (c f) -> p c f", c=NCH)
        w1_v = [VB(78848 + 8192 * c, FF) for c in range(NCH)]
        w2_v = [VB(144384 + 2048 * c, D) for c in range(NFCH)]

        ps_bf = es.enter_context(nc.psum_tensor("ps_bf", [P, 1024], BF16))
        psA = es.enter_context(nc.psum_tensor("psA", [P, 1024], F32))
        psB = es.enter_context(nc.psum_tensor("psB", [P, 1024], F32))
        psC = es.enter_context(nc.psum_tensor("psC", [P, 1024], F32))
        psD = es.enter_context(nc.psum_tensor("psD", [P, 512], F32))

        sem = {}
        for name in ["qx0", "qx1", "qw", "qg_rows", "qg_wk", "qg_wv",
                     "qg_wq", "qg_wo", "qg_w2", "qg_w1", "qg_wf", "qg_dsh", "qts",
                     "qtl0", "qtl1", "qs0", "qs1", "ql0", "ql1", "ql2",
                     "qg_xc0", "qg_xc1", "qx2",
                     "pe", "dv", "ac", "po"]:
            sem[name] = es.enter_context(nc.semaphore(name))
        ct = Ctr()

        lookup = marks if marks is not None else ct.marks

        def W(eng, key):
            if key in lookup:
                name, val = lookup[key]
                eng.wait_ge(sem[name], val)

        def I(inst, name, n=1, key=None):
            inst.then_inc(sem[name], n)
            ct.inc(name, n, key=key)

        def serial(eng, name):
            class _P:
                def __getattr__(self, attr):
                    if attr in ("wait_ge",):
                        return getattr(eng, attr)
                    fn = getattr(eng, attr)

                    def call(*a, key=None, **kw):
                        c = ct.c.get(name, 0)
                        if c > 0:
                            eng.wait_ge(sem[name], c)
                        inst = fn(*a, **kw)
                        inst.then_inc(sem[name], 1)
                        ct.inc(name, 1, key=key)
                        return inst
                    return call
            return _P()

        DMA_SEMS = ["qx0", "qx1", "qw", "qg_rows", "qg_wk", "qg_wv", "qg_wq",
                    "qg_wo", "qg_w2", "qg_w1", "qg_wf", "qg_dsh", "qts",
                    "qtl0", "qtl1", "qs0", "qs1", "ql0", "ql1", "ql2",
                    "qg_xc0", "qg_xc1", "qx2"]

        def drain_tail():
            with nc.Block("drain") as dblk:
                @dblk.sync
                def _(sy):
                    for name in DMA_SEMS:
                        if ct.c.get(name, 0) > 0:
                            sy.wait_ge(sem[name], ct.c[name])

        with nc.Block("init", no_gpsimd_drain=True) as blk:
            @blk.sync
            def _(sy):
                for dst, src in [(identb, idb_d), (identf, idf_d),
                                 (iplo, iplo_d), (iphi, iphi_d), (istk, istk_d),
                                 (onec, onec_d), (onesr, ones_d), (b1c, b1_d),
                                 (epszr, epsz_d),
                                 (g1r, g1_d), (be1r, be1_d)]:
                    sy.dma_start(dst[:, :], src).then_inc(sem["qw"], 16)
                    ct.inc("qw", 16)
                ct.marks["w_consts"] = ("qw", ct.c["qw"])

            @blk.gpsimd
            def _(gp):
                for dst, src in [(bqr, bq_d), (bkr, bk_d), (bvr, bv_d),
                                 (bor, bo_d), (b2r, b2_d)]:
                    gp.dma_start(dst[:, :], src).then_inc(sem["qg_rows"], 16)
                    ct.inc("qg_rows", 16)
                ct.marks["w_rows"] = ("qg_rows", ct.c["qg_rows"])
                for w in range(NCH):
                    gp.dma_start(wk_v[:, w, :], wk_d[w * P:(w + 1) * P, :]
                                 ).then_inc(sem["qg_wk"], 16)
                    ct.inc("qg_wk", 16, key=f"w_wk{w}")
                for w in range(NCH):
                    gp.dma_start(wv_v[:, w, :], wv_d[w * P:(w + 1) * P, :]
                                 ).then_inc(sem["qg_wv"], 16)
                    ct.inc("qg_wv", 16, key=f"w_wv{w}")
                for w in range(NCH):
                    gp.dma_start(wq_v[:, w, :], wq_d[w * P:(w + 1) * P, :]
                                 ).then_inc(sem["qg_wq"], 16)
                    ct.inc("qg_wq", 16, key=f"w_wq{w}")

            @blk.vector
            def _(v):
                v = serial(v, "dv")
                v.memset(ec[:, 0:1], EPS_LN)
                v.memset(km1[:, 0:1], -1.0)
                v.memset(kmh[:, 0:1], -0.5)
                v.memset(ksz[:, :, :], 0.0, key="zmem")

        if phases < 1:
            drain_tail()
            return nc, ct
        with nc.Block("passA", no_gpsimd_drain=True) as blk:
            xbufA = [xbuf[0], xbuf[1], xb3]

            def psK(t):
                return psA if t % 2 == 0 else psB

            def psV(t):
                return psB if t % 2 == 0 else psA

            @blk.sync
            def _(sy):
                for t in range(T):
                    W(sy, f"aV3_{t - 3}")
                    qn = ["qx0", "qx1", "qx2"][t % 3]
                    sy.dma_start(xbufA[t % 3][:, :],
                                 x_d[t * P:(t + 1) * P, :]
                                 ).then_inc(sem[qn], 16)
                    ct.inc(qn, 16, key=f"ax_{t}")

            @blk.gpsimd
            def _(gp):
                for st in range(NSUP):
                    if st < NCH:
                        gp.dma_start(wo_v[:, st, :], wo_d[st * P:(st + 1) * P, :]
                                     ).then_inc(sem["qg_wo"], 16)
                        ct.inc("qg_wo", 16, key=f"w_wo{st}")
                    for c in (2 * st, 2 * st + 1):
                        gp.dma_start(w2_v[c][:, :], w2_d[c * P:(c + 1) * P, :]
                                     ).then_inc(sem["qg_w2"], 16)
                        ct.inc("qg_w2", 16, key=f"w_w2{c}")

            @blk.scalar
            def _(sc):
                raw = sc
                sc = serial(sc, "ac")

                def zst(t):
                    if t >= T:
                        return
                    W(sc, f"ax_{t}")
                    W(sc, f"zV2_{t - 2}")
                    p_ = t % 2
                    sc.activation(tf32[:, :], xbufA[t % 3][:, :], AF.Copy,
                                  accum_out=cc[:, p_:p_ + 1],
                                  key=f"zA1_{t}")

                def za2(t):
                    if t >= T:
                        return
                    p_ = t % 2
                    W(sc, f"zV2_{t}")
                    sc.activation(r0c[:, t:t + 1], cc[:, 2 + p_:3 + p_],
                                  AF.Ln, bias=EPS_LN)
                    sc.activation(r0c[:, t:t + 1], r0c[:, t:t + 1], AF.Exp,
                                  scale=-0.5, key=f"zA2_{t}")

                zst(0)
                zst(1)
                za2(0)
                za2(1)
                for st in range(NSUP):
                    for sub in range(2):
                        t = 2 * st + sub
                        if t >= 1:
                            za2(t + 1)
                        zst(t + 2)
                        W(sc, f"aP2_{t}")
                        W(sc, f"aV5_{t - 1}")
                        sc.activation(scr2[:, :], psK(t)[:, :], AF.Relu,
                                      scale=-1.0)
                        sc.activation(scr2[:, :], scr2[:, :], AF.Exp,
                                      scale=-1.0, key=f"aA3_{t}")
                        W(sc, f"aP3_{t}")
                        W(sc, f"aP4_{t - 1}")
                        sc.activation(vbf[:, :], psV(t)[:, :], AF.Copy,
                                      key=f"aA4_{t}")
                    for g in range(4):
                        half = scr1[:, (g % 2) * 512:(g % 2) * 512 + 512]
                        pq = psA[:, (g % 2) * 512:(g % 2) * 512 + 512]
                        W(sc, f"aPQg_{st}_{g}")
                        if g >= 2:
                            W(sc, f"aVQg_{st}_{g - 2}")
                        else:
                            W(sc, f"aVQg_{st - 1}_{g + 2}")
                        sc.activation(half, pq, AF.Relu, scale=-1.0)
                        sc.activation(half, half, AF.Exp, scale=-1.0,
                                      key=f"aAQg_{st}_{g}")
                    W(raw, f"aVQg_{st}_3")
                    for sub in range(2):
                        raw.dma_start(
                            qt_d[2 * st + sub],
                            qts4[:, sub, :, :].rearrange("p c s -> p (c s)")
                        ).then_inc(sem["qts"], 16)
                        ct.inc("qts", 16, key=f"aqts_{st}_{sub}")

            @blk.vector
            def _(v):
                v = serial(v, "dv")
                srcbfP = [srcbf, srcbf2]

                def emit_v3(t):
                    if t >= T:
                        return
                    W(v, f"ax_{t}")
                    W(v, f"aP1_{t - 2}")
                    W(v, f"zA2_{t}")
                    v.tensor_scalar(srcbfP[t % 2][:, :], xbufA[t % 3][:, :],
                                    m0c[:, t:t + 1], r0c[:, t:t + 1],
                                    ALU.subtract, ALU.mult,
                                    key=f"aV3_{t}")

                def zsq(t):
                    if t >= T:
                        return
                    p_ = t % 2
                    W(v, f"ax_{t}")
                    v.scalar_tensor_tensor(zdum[:, :], xbufA[t % 3][:, :],
                                           1.0, xbufA[t % 3][:, :],
                                           ALU.mult, ALU.mult,
                                           accum_out=cc[:, 2 + p_:3 + p_],
                                           key=f"zSq_{t}")

                def zcols(t):
                    if t >= T:
                        return
                    p_ = t % 2
                    W(v, f"zA1_{t}")
                    v.tensor_scalar(m0c[:, t:t + 1], cc[:, p_:p_ + 1],
                                    INV_D, None, ALU.mult)
                    v.tensor_tensor(cc[:, p_:p_ + 1], m0c[:, t:t + 1],
                                    m0c[:, t:t + 1], ALU.mult)
                    v.tensor_scalar(cc[:, 2 + p_:3 + p_],
                                    cc[:, 2 + p_:3 + p_], INV_D,
                                    cc[:, p_:p_ + 1], ALU.mult,
                                    ALU.subtract, key=f"zV2_{t}")

                def emit_v4(t):
                    sub = t % 2
                    W(v, f"aP1_{t}")
                    v.tensor_copy(srcTs[:, 0:4, sub * P:(sub + 1) * P],
                                  ps_bf[:, 0:512].rearrange(
                                      "p (c s) -> p c s", c=4),
                                  key=f"aV4a_{t}")
                    v.tensor_copy(srcTs[:, 4:8, sub * P:(sub + 1) * P],
                                  ps_bf[:, 512:1024].rearrange(
                                      "p (c s) -> p c s", c=4),
                                  key=f"aV4_{t}")

                def emit_v5(t):
                    if t < 0:
                        return
                    W(v, f"aA3_{t}")
                    W(v, f"aP4_{t - 1}")
                    v.scalar_tensor_tensor(kbf[:, :], psK(t)[:, :], 0.0,
                                           scr2[:, :], ALU.max, ALU.add,
                                           key=f"aV5_{t}")

                zsq(0)
                zcols(0)
                zsq(1)
                zcols(1)
                emit_v3(0)
                emit_v3(1)
                for st in range(NSUP):
                    for sub in range(2):
                        t = 2 * st + sub
                        emit_v4(t)
                        emit_v5(t - 1)
                        if t >= 1:
                            emit_v3(t + 1)
                        zsq(t + 2)
                        zcols(t + 2)
                    for g in range(4):
                        half = scr1[:, (g % 2) * 512:(g % 2) * 512 + 512]
                        pq = psA[:, (g % 2) * 512:(g % 2) * 512 + 512]
                        W(v, f"aAQg_{st}_{g}")
                        if g == 0:
                            W(v, f"aqts_{st - 1}_1")
                        for u_ in range(2):
                            v.scalar_tensor_tensor(
                                qts4[:, u_, 2 * g:2 * g + 2, :],
                                pq.rearrange("p (c u s) -> p c u s",
                                             c=2, u=2)[:, :, u_, :],
                                0.0,
                                half.rearrange("p (c u s) -> p c u s",
                                               c=2, u=2)[:, :, u_, :],
                                ALU.max, ALU.add,
                                key=(f"aVQg_{st}_{g}" if u_ == 1 else None))
                emit_v5(T - 1)

            @blk.tensor
            def _(te):
                W(te, "w_consts")

                def emit_p4(t):
                    first = (t == 0)
                    last = (t == T - 1)
                    W(te, f"aV5_{t}")
                    W(te, f"aA4_{t}")
                    for h in range(H):
                        col = (h % 2) * 512 + (h // 2) * DH
                        te.matmul(psC[0:DH, col:col + DH],
                                  kbf[:, h * DH:(h + 1) * DH],
                                  vbf[:, h * DH:(h + 1) * DH],
                                  start=(first and h < 2),
                                  stop=(last and h >= H - 2))
                    for h in range(H):
                        colz = (h % 2) * 8 + (h // 2)
                        ii = te.matmul(psD[0:DH, colz:colz + 1],
                                       kbf[:, h * DH:(h + 1) * DH],
                                       onec[:, 0:1],
                                       start=(first and h == 0),
                                       stop=(last and h == H - 1))
                    I(ii, "pe", key=f"aP4_{t}")

                for st in range(NSUP):
                    for sub in range(2):
                        t = 2 * st + sub
                        W(te, f"aV3_{t}")
                        W(te, f"aV4_{t - 1}")
                        sbp = srcbf if t % 2 == 0 else srcbf2
                        for c in range(NCH):
                            ii = te.transpose(ps_bf[:, c * P:(c + 1) * P],
                                              sbp[:, c * P:(c + 1) * P],
                                              identb[:, :])
                        I(ii, "pe", key=f"aP1_{t}")
                        W(te, f"aV4a_{t}")
                        W(te, f"w_wk{NCH - 1}")
                        W(te, "w_rows")
                        if sub == 0:
                            W(te, f"aVQg_{st - 1}_3")
                        else:
                            W(te, f"aA4_{t - 1}")
                        for w in range(NCH):
                            if w == 4:
                                W(te, f"aV4_{t}")
                            for nh in range(2):
                                te.matmul(psK(t)[:, nh * 512:(nh + 1) * 512],
                                          srcTs[:, w, sub * P:(sub + 1) * P],
                                          wk_v[:, w, nh * 512:(nh + 1) * 512],
                                          start=(w == 0), stop=False)
                        for nh in range(2):
                            ii = te.matmul(psK(t)[:, nh * 512:(nh + 1) * 512],
                                           onesr[:, 0:P],
                                           bkr[:, nh * 512:(nh + 1) * 512],
                                           start=False, stop=True)
                        I(ii, "pe", key=f"aP2_{t}")
                        W(te, f"w_wv{NCH - 1}")
                        if sub == 0:
                            W(te, f"aVQg_{st - 1}_3")
                            W(te, f"aV5_{t - 1}")
                        else:
                            W(te, f"aV5_{t - 1}")
                        for w in range(NCH):
                            for nh in range(2):
                                te.matmul(psV(t)[:, nh * 512:(nh + 1) * 512],
                                          srcTs[:, w, sub * P:(sub + 1) * P],
                                          wv_v[:, w, nh * 512:(nh + 1) * 512],
                                          start=(w == 0), stop=False)
                        for nh in range(2):
                            ii = te.matmul(psV(t)[:, nh * 512:(nh + 1) * 512],
                                           onesr[:, 0:P],
                                           bvr[:, nh * 512:(nh + 1) * 512],
                                           start=False, stop=True)
                        I(ii, "pe", key=f"aP3_{t}")
                        if t >= 1:
                            emit_p4(t - 1)
                    W(te, f"w_wq{NCH - 1}")
                    for g in range(4):
                        pqb = psA[:, (g % 2) * 512:(g % 2) * 512 + 512]
                        if g >= 2:
                            W(te, f"aVQg_{st}_{g - 2}")
                        elif g == 0:
                            W(te, f"aV5_{2 * st}")
                            W(te, f"aA4_{2 * st + 1}")
                        for cc_ in range(2):
                            c = 2 * g + cc_
                            pq = pqb[:, cc_ * 256:cc_ * 256 + 256]
                            te.matmul(pq, bqr[:, c * P:(c + 1) * P],
                                      onesr[:, 0:256],
                                      start=(cc_ == 0), stop=False)
                            for w in range(NCH):
                                ii = te.matmul(pq,
                                               wq_v[:, w, c * P:(c + 1) * P],
                                               srcTs[:, w, :], start=False,
                                               stop=(w == NCH - 1
                                                     and cc_ == 1))
                            if cc_ == 1:
                                I(ii, "pe", key=f"aPQg_{st}_{g}")
                emit_p4(T - 1)

        if phases < 2:
            drain_tail()
            return nc, ct
        with nc.Block("finalize", no_gpsimd_drain=True) as blk:
            @blk.vector
            def _(v):
                v = serial(v, "dv")
                v.tensor_copy(kvsb[:, :], psC[0:DH, :], key="fkv")
                v.tensor_copy(ksb[:, :], psD[0:DH, 0:H], key="fks")
                W(v, "fpad")
                v.tensor_copy(kvz[:, :], psA[:, :], key="fkvz")
                for h in range(H):
                    colz = (h % 2) * 8 + (h // 2)
                    v.tensor_copy(ksz[:, h // 2, h:h + 1],
                                  psB[:, colz:colz + 1],
                                  key=("fksz" if h == H - 1 else None))

            @blk.tensor
            def _(te):
                W(te, "fks")
                for h in range(H):
                    col = (h % 2) * 512 + (h // 2) * DH
                    pad = iplo if h % 2 == 0 else iphi
                    te.matmul(psA[:, h * DH:(h + 1) * DH], pad[:, :],
                              kvsb[:, col:col + DH],
                              start=(h in (0, 8)), stop=(h in (7, 15)))
                te.matmul(psB[:, 0:8], iplo[:, :], ksb[:, 0:8], start=True,
                          stop=False)
                ii = te.matmul(psB[:, 8:16], iphi[:, :], ksb[:, 8:16],
                               start=False, stop=True)
                I(ii, "pe", key="fpad")

        if phases < 3:
            drain_tail()
            return nc, ct
        with nc.Block("passB1", no_gpsimd_drain=True) as blk:
            srcb1 = [scr1, scr2]
            zcolsP = [zcols, zcols2]

            def psO(t):
                return psA

            def psZ(t):
                return psD[:, 0:H] if t % 2 == 0 else psD[:, H:2 * H]

            @blk.sync
            def _(sy):
                for t in range(T):
                    W(sy, f"bPo_{t - 2}")
                    sy.dma_start(qtbuf[t % 2][:, :, :],
                                 qt_d[t].rearrange("p (c s) -> p c s", c=NCH)
                                 ).then_inc(sem[f"qtl{t % 2}"], 16)
                    ct.inc(f"qtl{t % 2}", 16, key=f"bqtl_{t}")
                    W(sy, f"bVsrc_{t - 2}")
                    sy.dma_start(xbuf[t % 2][:, :], x_d[t * P:(t + 1) * P, :]
                                 ).then_inc(sem[f"qx{t % 2}"], 16)
                    ct.inc(f"qx{t % 2}", 16, key=f"bx_{t}")


            @blk.gpsimd
            def _(gp):
                def ptt(u):
                    W(gp, f"bVx1v_{u}")
                    ii = gp.tensor_tensor(stf32[u % 2][:, :],
                                          stf32[u % 2][:, :], be1r[:, :],
                                          ALU.add)
                    I(ii, "po", key=f"bVx1_{u}")
                    ct.marks[f"bG1_{u}"] = ct.marks[f"bVx1_{u}"]

                for t in range(T):
                    if t >= 2:
                        ptt(t - 2)
                    if t % 3 == 0 and t // 3 < NCH:
                        w = t // 3
                        W(gp, f"bVt_{min(3 * w, T - 1)}")
                        gp.dma_start(w1_v[w][:, :],
                                     w1_d[w * P:(w + 1) * P, :]
                                     ).then_inc(sem["qg_w1"], 16)
                        ct.inc("qg_w1", 16, key=f"w_w1{w}")
                ptt(T - 2)
                ptt(T - 1)
                W(gp, "bst1_0")
                W(gp, f"bVtq_{T - 1}")
                gp.dma_start(x1bf[:, :], x1_d[0]).then_inc(sem["qg_xc0"], 16)
                ct.inc("qg_xc0", 16, key="cVcst_0")
                W(gp, "bst1_1")
                gp.dma_start(x1bfB[:, :], x1_d[1]).then_inc(sem["qg_xc1"], 16)
                ct.inc("qg_xc1", 16, key="cVcst_1")
                W(gp, f"bVt_{T - 1}")
                gp.dma_start(wfstage[:, :], wf_d[0:P, :]
                             ).then_inc(sem["qg_wf"], 16)
                ct.inc("qg_wf", 16, key="w_wf0")

            @blk.tensor
            def _(te):
                def stage_head(t):
                    zps = psZ(t)
                    W(te, f"bqtl_{t}")
                    W(te, f"bVz2_{t - 2}")
                    if t == 0:
                        W(te, "fksz")
                    if t == 1:
                        W(te, "fkv")
                    for c in range(NCH):
                        te.matmul(zps, qtbuf[t % 2][:, c, :], ksz[:, c, :],
                                  start=(c == 0), stop=False)
                    ii = te.matmul(zps, onesr[:, 0:P], epszr[:, :],
                                   start=False, stop=True)
                    I(ii, "pe", key=f"bPZ_{t}")
                    W(te, f"bAs_{t - 1}")
                    if t == 0:
                        W(te, "fkvz")
                    for h in range(H):
                        ii = te.matmul(psO(t)[:, h * DH:(h + 1) * DH],
                                       qtbuf[t % 2][:, h // 2, :],
                                       kvz[:, h * DH:(h + 1) * DH],
                                       start=(h in (0, 8)), stop=(h in (7, 15)))
                    I(ii, "pe", key=f"bPo_{t}")

                def pe_PoT(u):
                    W(te, f"bAs_{u}")
                    W(te, f"bVoT_{u - 1}")
                    for c in range(NCH):
                        ii = te.transpose(ps_bf[:, c * P:(c + 1) * P],
                                          obf[:, c * P:(c + 1) * P],
                                          identb[:, :])
                    I(ii, "pe", key=f"bPoT_{u}")

                def pe_Pat(u):
                    pat = psB if u % 2 == 0 else psC
                    W(te, f"bVoT_{u}")
                    W(te, f"bVt_{u - 2}")
                    if u == 1:
                        W(te, "fkv")
                    W(te, f"w_wo{NCH - 1}")
                    for w in range(NCH):
                        for nh in range(2):
                            te.matmul(pat[:, nh * 512:(nh + 1) * 512],
                                      oTv[:, w * P:(w + 1) * P],
                                      wo_v[:, w, nh * 512:(nh + 1) * 512],
                                      start=(w == 0), stop=False)
                    for nh in range(2):
                        ii = te.matmul(pat[:, nh * 512:(nh + 1) * 512],
                                       onesr[:, 0:P],
                                       bor[:, nh * 512:(nh + 1) * 512],
                                       start=False, stop=True)
                    I(ii, "pe", key=f"bPat_{u}")

                for t in range(T):
                    if t >= 1:
                        pe_PoT(t - 1)
                    stage_head(t)
                    if t >= 1:
                        pe_Pat(t - 1)
                pe_PoT(T - 1)
                pe_Pat(T - 1)

            @blk.vector
            def _(v):
                v = serial(v, "dv")
                bcol = [(4, 5, 6, 7, 8, 9), (10, 11, 12, 13, 14, 15)]

                def vec_head(t):
                    W(v, f"bx_{t}")
                    W(v, f"bVt_{t - 2}")
                    v.tensor_scalar(srcb1[t % 2][:, :], xbuf[t % 2][:, :],
                                    m0c[:, t:t + 1], r0c[:, t:t + 1],
                                    ALU.subtract, ALU.mult, key=f"bVsrc_{t}")
                    W(v, f"bPZ_{t}")
                    W(v, f"bAs_{t - 2}")
                    v.reciprocal(zcolsP[t % 2][:, :], psZ(t),
                                 key=f"bVz2_{t}")
                    W(v, f"bPo_{t}")
                    W(v, f"bPoT_{t - 1}")
                    W(v, f"bVtq_{t - 2}")
                    zc = zcolsP[t % 2]
                    zb = bass.AP(zc.tensor, zc.offset,
                                 [list(zc.ap[0]), [1, H], [0, DH]])
                    v.tensor_tensor(
                        obf[:, :].rearrange("p (h j) -> p h j", h=H),
                        psO(t)[:, :].rearrange("p (h j) -> p h j", h=H),
                        zb, ALU.mult, key=f"bAs_{t}")

                def vec_t(u):
                    cs = bcol[u % 2]
                    pat = psB if u % 2 == 0 else psC
                    W(v, f"bPat_{u}")
                    W(v, f"bVtq_{u - 1}")
                    v.scalar_tensor_tensor(tf32[:, :], pat[:, :], 1.0,
                                           srcb1[u % 2][:, :], ALU.mult,
                                           ALU.add,
                                           accum_out=cc[:, cs[0]:cs[0] + 1],
                                           key=f"bVt_{u}")

                def vec_x1(u):
                    cs = bcol[u % 2]
                    W(v, f"bVtq_{u}")
                    v.tensor_scalar(cc[:, cs[2]:cs[2] + 1],
                                    cc[:, cs[0]:cs[0] + 1], INV_D, None,
                                    ALU.mult)
                    v.tensor_tensor(cc[:, cs[3]:cs[3] + 1],
                                    cc[:, cs[2]:cs[2] + 1],
                                    cc[:, cs[2]:cs[2] + 1], ALU.mult)
                    v.tensor_scalar(cc[:, cs[4]:cs[4] + 1],
                                    cc[:, cs[1]:cs[1] + 1], INV_D,
                                    cc[:, cs[3]:cs[3] + 1], ALU.mult,
                                    ALU.subtract, key=f"bVc_{u}")
                    W(v, f"bAr1_{u}")
                    if u < 2:
                        W(v, f"aqts_{NSUP - 1}_1")
                    W(v, f"bG1_{u - 2}")
                    W(v, f"bst1_{u - 2}")
                    v.scalar_tensor_tensor(stf32[u % 2][:, :], tf32[:, :],
                                           cc[:, cs[2]:cs[2] + 1], g1r[:, :],
                                           ALU.subtract, ALU.mult)
                    v.tensor_scalar(stf32[u % 2][:, :], stf32[u % 2][:, :],
                                    cc[:, cs[5]:cs[5] + 1], None, ALU.mult,
                                    key=f"bVx1v_{u}")

                for t in range(T):
                    vec_head(t)
                    if t >= 2:
                        vec_x1(t - 2)
                    if t >= 1:
                        vec_t(t - 1)
                vec_x1(T - 2)
                vec_t(T - 1)
                vec_x1(T - 1)

            @blk.scalar
            def _(sc):
                raw = sc
                sc = serial(sc, "ac")
                bcol = [(4, 5, 6, 7, 8, 9), (10, 11, 12, 13, 14, 15)]

                def sc_VoT(u):
                    W(sc, f"bPoT_{u}")
                    sc.copy(oTv[:, :], ps_bf[:, :], key=f"bVoT_{u}")

                def sc_Sq(u):
                    cs = bcol[u % 2]
                    W(sc, f"bVt_{u}")
                    sc.activation(obf[:, :], tf32[:, :], AF.Square,
                                  accum_out=cc[:, cs[1]:cs[1] + 1],
                                  key=f"bVtq_{u}")
                    W(sc, f"bVc_{u}")
                    sc.activation(cc[:, cs[5]:cs[5] + 1],
                                  cc[:, cs[4]:cs[4] + 1], AF.Ln,
                                  bias=EPS_LN)
                    sc.activation(cc[:, cs[5]:cs[5] + 1],
                                  cc[:, cs[5]:cs[5] + 1], AF.Exp,
                                  scale=-0.5, key=f"bAr1_{u}")

                def sc_store(u):
                    W(raw, f"bG1_{u}")
                    raw.dma_start(x1_d[u], stf32[u % 2][:, :]
                                  ).then_inc(sem[f"qs{u % 2}"], 16)
                    ct.inc(f"qs{u % 2}", 16, key=f"bst1_{u}")

                for t in range(T):
                    if t >= 1:
                        sc_VoT(t - 1)
                    if t >= 2:
                        sc_Sq(t - 2)
                    if t >= 3:
                        sc_store(t - 3)
                sc_VoT(T - 1)
                sc_Sq(T - 2)
                sc_Sq(T - 1)
                for u in (T - 3, T - 2, T - 1):
                    sc_store(u)

        if phases < 4:
            drain_tail()
            return nc, ct
        with nc.Block("passB2a", no_gpsimd_drain=True) as blk:
            @blk.sync
            def _(sy):
                sy.dma_start(g2r[:, :], g2_d).then_inc(sem["qw"], 16)
                ct.inc("qw", 16)
                sy.dma_start(be2r[:, :], be2_d).then_inc(sem["qw"], 16)
                ct.inc("qw", 16, key="w_g2")
                for t in range(T):
                    W(sy, f"cVuP_{t - 3}")
                    W(sy, f"bst1_{t}")
                    sy.dma_start(ldf3[t % 3][:, :], x1_d[t]
                                 ).then_inc(sem[f"ql{t % 3}"], 16)
                    ct.inc(f"ql{t % 3}", 16, key=f"clx1_{t}")


            @blk.gpsimd
            def _(gp):
                def xc(t):
                    if t >= T:
                        return
                    W(gp, f"bst1_{t}")
                    W(gp, f"cPt_{t - 2}")
                    xb = x1bf if t % 2 == 0 else x1bfB
                    gp.dma_start(xb[:, :], x1_d[t]
                                 ).then_inc(sem[f"qg_xc{t % 2}"], 16)
                    ct.inc(f"qg_xc{t % 2}", 16, key=f"cVcst_{t}")

                gp.dma_start(dshb[:, :], dsh_d).then_inc(sem["qg_dsh"], 16)
                ct.inc("qg_dsh", 16, key="w_dsh")
                for st in range(NSUP):
                    if 1 <= st < NCH:
                        W(gp, f"cVW_{st - 1}")
                        gp.dma_start(wfstage[:, :],
                                     wf_d[st * P:(st + 1) * P, :]
                                     ).then_inc(sem["qg_wf"], 16)
                        ct.inc("qg_wf", 16, key=f"w_wf{st}")
                    xc(2 * st + 2)
                    xc(2 * st + 3)

            @blk.tensor
            def _(te):
                psDbf = psD.ap().bitcast(BF16)
                for st in range(NSUP):
                    for sub in range(2):
                        t = 2 * st + sub
                        pt = ps_bf[:, :] if sub == 0 else psDbf[:, 0:1024]
                        xb = x1bf if t % 2 == 0 else x1bfB
                        W(te, f"cVcst_{t}")
                        W(te, f"cVxT_{t - 2}")
                        if sub == 0:
                            W(te, f"cVW_{st - 1}")
                        for c in range(NCH):
                            ii = te.transpose(pt[:, c * P:(c + 1) * P],
                                              xb[:, c * P:(c + 1) * P],
                                              identb[:, :])
                        I(ii, "pe", key=f"cPt_{t}")
                    if st < NCH:
                        W(te, f"w_wf{st}")
                        W(te, f"cVxT_{2 * st}")
                        for c in range(NCH):
                            ii = te.transpose(ps_bf[:, c * P:(c + 1) * P],
                                              wfstage[:, c * P:(c + 1) * P],
                                              identb[:, :])
                        I(ii, "pe", key=f"cPW_{st}")
                    W(te, f"cVxT_{2 * st + 1}")
                    if st == 0:
                        W(te, f"w_w1{NCH - 1}")
                        W(te, f"w_w2{NFCH - 1}")
                    x1t = x1Ts if st % 2 == 0 else x1Ts2

                    def emit_ph(c):
                        ph = psC[:, 0:256] if c % 2 == 0 else psC[:, 512:768]
                        if c >= 2:
                            W(te, f"cAg_{st}_{c - 2}")
                        else:
                            W(te, f"cAg_{st - 1}_{NFCH - 2 + c}")
                        for w in range(NCH):
                            ii = te.matmul(ph, w1_v[w][:, c * P:(c + 1) * P],
                                           x1t[:, w, :], start=(w == 0),
                                           stop=(w == NCH - 1))
                        I(ii, "pe", key=f"cPh_{st}_{c}")

                    def emit_py(c):
                        W(te, f"cAg_{st}_{c}")
                        for sub in range(2):
                            psY = psA if sub == 0 else psB
                            if c == 0:
                                W(te, f"cVuP_{2 * (st - 1) + sub}")
                            for nh in range(2):
                                ii = te.matmul(
                                    psY[:, nh * 512:(nh + 1) * 512],
                                    hT[c % 2][:, sub * P:(sub + 1) * P],
                                    w2_v[c][:, nh * 512:(nh + 1) * 512],
                                    start=(c == 0), stop=False)
                        I(ii, "pe", key=f"cPy_{st}_{c}")

                    for c in range(NFCH):
                        emit_ph(c)
                        if c >= 1:
                            emit_py(c - 1)
                    emit_py(NFCH - 1)
                    for sub in range(2):
                        psY = psA if sub == 0 else psB
                        for nh in range(2):
                            ii = te.matmul(psY[:, nh * 512:(nh + 1) * 512],
                                           onesr[:, 0:P],
                                           b2r[:, nh * 512:(nh + 1) * 512],
                                           start=False, stop=True)
                        I(ii, "pe", key=f"cPyb_{st}_{sub}")

            @blk.vector
            def _(v):
                v = serial(v, "dv")

                psDbf_v = psD.ap().bitcast(BF16)

                def cvxt(t):
                    if t >= T:
                        return
                    st = t // 2
                    sub = t % 2
                    pt = ps_bf[:, :] if sub == 0 else psDbf_v[:, 0:1024]
                    W(v, f"cPt_{t}")
                    W(v, f"cPh_{st - 2}_{NFCH - 1}")
                    x1tv = x1Ts if st % 2 == 0 else x1Ts2
                    v.tensor_copy(x1tv[:, :, sub * P:(sub + 1) * P],
                                  pt.rearrange("p (c s) -> p c s", c=NCH),
                                  key=f"cVxT_{t}")

                def cvw(st):
                    pass

                def cvu(t):
                    st = t // 2
                    sub = t % 2
                    psY = psA if sub == 0 else psB
                    W(v, f"cPyb_{st}_{sub}")
                    W(v, f"clx1_{t}")
                    if t < 2:
                        W(v, f"bst1_{T - 2 + t}")
                    W(v, f"cstu_{t - 2}")
                    W(v, f"cSq_{t - 2}")
                    v.scalar_tensor_tensor(stf32[t % 2][:, :], psY[:, :],
                                           1.0, ldf3[t % 3][:, :],
                                           ALU.mult, ALU.add,
                                           accum_out=usc[:, t:t + 1],
                                           key=f"cVuP_{t}")

                for st in range(NSUP):
                    if st >= 1:
                        cvu(2 * st - 2)
                    cvxt(2 * st)
                    cvxt(2 * st + 1)
                    if st >= 1:
                        cvu(2 * st - 1)
                    cvw(st)
                cvu(T - 2)
                cvu(T - 1)

            @blk.scalar
            def _(sc):
                raw = sc
                sc = serial(sc, "ac")


                def scw(st):
                    if st < NCH:
                        W(sc, f"cPW_{st}")
                        sc.copy(wfT[:, :, st * P:(st + 1) * P],
                                ps_bf[:, :].rearrange("p (c s) -> p c s",
                                                      c=NCH),
                                key=f"cVW_{st}")

                for st in range(NSUP):
                    scw(st)
                    for c in range(NFCH):
                        ph = psC[:, 0:256] if c % 2 == 0 else psC[:, 512:768]
                        W(sc, f"cPh_{st}_{c}")
                        if c >= 2:
                            W(sc, f"cPy_{st}_{c - 2}")
                        else:
                            W(sc, f"cPy_{st - 1}_{NFCH - 2 + c}")
                        sc.activation(hT[c % 2][:, :], ph, AF.Gelu,
                                        bias=b1c[:, c:c + 1], key=f"cAg_{st}_{c}")
                    for sub in range(2):
                        t = 2 * st + sub
                        W(sc, f"cVuP_{t}")
                        sc.activation(x1bf2[:, :], stf32[t % 2][:, :],
                                      AF.Square, accum_out=uqc[:, t:t + 1],
                                      key=f"cSq_{t}")
                        W(raw, f"cSq_{t}")
                        qn = ["qs0", "qs1", "qx2"][t % 3]
                        raw.dma_start(u_d[t], stf32[t % 2][:, :]
                                      ).then_inc(sem[qn], 16)
                        ct.inc(qn, 16, key=f"cstu_{t}")

        if phases < 5:
            drain_tail()
            return nc, ct
        with nc.Block("passB2b", no_gpsimd_drain=True) as blk:
            psBbf = psB.ap().bitcast(BF16)
            scrAP = [scrA, scrA2]
            xAvP = [xAv, xAv2]
            nbfP = [nbf, nbf2]
            nTP = [nTv, nT2]
            poolTP = [poolT, poolT2]
            xAv4P = [xAv, xAv2, xAv3, xAv4]

            def i4(t):
                return ((t // 2) % 2) * 2 + (t % 2)
            colset = [(4, 5, 14, 15, 7, 12), (0, 1, 2, 3, 10, 11)]

            @blk.sync
            def _(sy):
                for t in range(T):
                    W(sy, f"dVg2_{t - 2}")
                    W(sy, f"cstu_{t}")
                    sy.dma_start(ldf32[t % 2][:, :], u_d[t]
                                 ).then_inc(sem[f"ql{t % 2}"], 16)
                    ct.inc(f"ql{t % 2}", 16, key=f"dlu_{t}")
                    W(sy, f"dVxA_{t - 2}")
                    sy.dma_start(xbuf[t % 2][:, :], x_d[t * P:(t + 1) * P, :]
                                 ).then_inc(sem[f"qx{t % 2}"], 16)
                    ct.inc(f"qx{t % 2}", 16, key=f"dx_{t}")

            @blk.vector
            def _(v):
                v = serial(v, "dv")
                v.tensor_scalar(m0c[:, :], usc[:, :], INV_D, None, ALU.mult)
                v.tensor_tensor(usc[:, :], m0c[:, :], m0c[:, :], ALU.mult)
                v.tensor_scalar(uqc[:, :], uqc[:, :], INV_D, None, ALU.mult)
                v.tensor_tensor(uqc[:, :], uqc[:, :], usc[:, :], ALU.subtract,
                                key="dVln2")

                def vg2(t):
                    cs = colset[t % 2]
                    if t == 0:
                        W(v, "w_g2")
                        W(v, "dAr2")
                    W(v, f"dlu_{t}")
                    v.scalar_tensor_tensor(scrAP[t % 2][:, :],
                                           ldf32[t % 2][:, :],
                                           m0c[:, t:t + 1], g2r[:, :],
                                           ALU.subtract, ALU.mult,
                                           key=f"dVg2_{t}")

                def vcols(u):
                    cs = colset[u % 2]
                    W(v, f"dSq_{u}")
                    v.tensor_scalar(cc[:, cs[2]:cs[2] + 1],
                                    cc[:, cs[0]:cs[0] + 1], INV_D, None,
                                    ALU.mult)
                    v.tensor_tensor(cc[:, cs[4]:cs[4] + 1],
                                    cc[:, cs[2]:cs[2] + 1],
                                    cc[:, cs[2]:cs[2] + 1], ALU.mult)
                    v.tensor_scalar(cc[:, cs[3]:cs[3] + 1],
                                    cc[:, cs[1]:cs[1] + 1], INV_D,
                                    cc[:, cs[4]:cs[4] + 1], ALU.mult,
                                    ALU.subtract, key=f"dVc3_{u}")

                def vxa(t):
                    cs = colset[t % 2]
                    W(v, f"dXbe_{t}")
                    v.scalar_tensor_tensor(xAv4P[i4(t)][:, :],
                                           scrAP[t % 2][:, :],
                                           r0c[:, t:t + 1],
                                           xbuf[t % 2][:, :], ALU.mult,
                                           ALU.add,
                                           accum_out=cc[:, cs[0]:cs[0] + 1],
                                           key=f"dVxA_{t}")

                def vn(u):
                    cs = colset[u % 2]
                    W(v, f"dAr3_{u}")
                    W(v, f"dPn_{u - 2}")
                    v.tensor_scalar(nbfP[u % 2][:, :], xAv4P[i4(u)][:, :],
                                    cc[:, cs[2]:cs[2] + 1],
                                    cc[:, cs[5]:cs[5] + 1], ALU.subtract,
                                    ALU.mult, key=f"dVn_{u}")

                def vout(u):
                    ps = psA if u % 2 == 0 else psC
                    W(v, f"dPr_{u}")
                    if u < 2:
                        W(v, f"cstu_{T - 2 + u}")
                    W(v, f"dsto_{u - 2}")
                    v.tensor_tensor(stf32[u % 2][:, :], ps[:, :],
                                    xAv4P[i4(u)][:, :], ALU.add,
                                    key=f"dVout_{u}")

                for p in range(T // 2):
                    t, t2 = 2 * p, 2 * p + 1
                    vg2(t)
                    vg2(t2)
                    if p >= 1:
                        vcols(t - 2)
                        vcols(t2 - 2)
                    if p >= 2:
                        vout(t - 4)
                        vout(t2 - 4)
                    if p >= 1:
                        vn(t - 2)
                        vn(t2 - 2)
                    vxa(t)
                    vxa(t2)
                vcols(T - 2)
                vcols(T - 1)
                vout(T - 4)
                vout(T - 3)
                vn(T - 2)
                vn(T - 1)
                vout(T - 2)
                vout(T - 1)

            @blk.scalar
            def _(sc):
                raw = sc
                sc = serial(sc, "ac")
                W(sc, "dVln2")
                sc.activation(r0c[:, :], uqc[:, :], AF.Ln, bias=EPS_LN)
                sc.activation(r0c[:, :], r0c[:, :], AF.Exp, scale=-0.5,
                              key="dAr2")
                def ar3(u):
                    cs = colset[u % 2]
                    W(sc, f"dVc3_{u}")
                    sc.activation(cc[:, cs[5]:cs[5] + 1],
                                  cc[:, cs[3]:cs[3] + 1], AF.Ln,
                                  bias=EPS_LN)
                    sc.activation(cc[:, cs[5]:cs[5] + 1],
                                  cc[:, cs[5]:cs[5] + 1], AF.Exp,
                                  scale=-0.5, key=f"dAr3_{u}")

                def nT_(u):
                    pn = ps_bf[:, :] if u % 2 == 0 else psBbf[:, 0:1024]
                    W(sc, f"dPr_{u - 2}")
                    W(sc, f"dPn_{u}")
                    sc.copy(nTP[u % 2][:, :, :],
                            pn.rearrange("p (c s) -> p c s", c=NCH),
                            key=f"dVnT_{u}")

                def pool_(u):
                    pp = psD[0:DH, 0:P] if u % 2 == 0 \
                        else psB[0:DH, 512:640]
                    W(sc, f"dPp_{u}")
                    sc.copy(poolTP[u % 2][:, :], pp, key=f"dVpo_{u}")

                def sq(t):
                    cs = colset[t % 2]
                    W(sc, f"dVxA_{t}")
                    sc.activation(nbfP[t % 2][:, :], xAv4P[i4(t)][:, :],
                                  AF.Square,
                                  accum_out=cc[:, cs[1]:cs[1] + 1],
                                  key=f"dSq_{t}")

                def sto(u):
                    W(raw, f"dVout_{u}")
                    qn = ["qs0", "qs1", "qx2"][u % 3]
                    raw.dma_start(out_d[u * P:(u + 1) * P, :],
                                  stf32[u % 2][:, :]
                                  ).then_inc(sem[qn], 16)
                    ct.inc(qn, 16, key=f"dsto_{u}")

                for p in range(T // 2):
                    t, t2 = 2 * p, 2 * p + 1
                    if p >= 1:
                        ar3(t - 2)
                        ar3(t2 - 2)
                        nT_(t - 2)
                        nT_(t2 - 2)
                        pool_(t - 2)
                        pool_(t2 - 2)
                    if p >= 2:
                        sto(t - 4)
                        sto(t2 - 4)
                    sq(t)
                    sq(t2)
                ar3(T - 2)
                ar3(T - 1)
                nT_(T - 2)
                nT_(T - 1)
                pool_(T - 2)
                pool_(T - 1)
                for u in (T - 4, T - 3, T - 2, T - 1):
                    sto(u)

            @blk.gpsimd
            def _(gp):
                sgp = serial(gp, "po")
                for t in range(T):
                    W(gp, f"dx_{t}")
                    if t == 0:
                        W(gp, "w_g2")
                    sgp.tensor_tensor(xbuf[t % 2][:, :], xbuf[t % 2][:, :],
                                      be2r[:, :], ALU.add, key=f"dXbe_{t}")

            @blk.tensor
            def _(te):
                def pn_(u):
                    pn = ps_bf[:, :] if u % 2 == 0 else psBbf[:, 0:1024]
                    W(te, f"dVn_{u}")
                    W(te, f"dVnT_{u - 2}")
                    for c in range(NCH):
                        ii = te.transpose(pn[:, c * P:(c + 1) * P],
                                          nbfP[u % 2][:, c * P:(c + 1) * P],
                                          identb[:, :])
                    I(ii, "pe", key=f"dPn_{u}")

                def pp_(u):
                    pp = psD[0:DH, 0:P] if u % 2 == 0 else psB[0:DH, 512:640]
                    W(te, f"dVnT_{u}")
                    W(te, f"dVpo_{u - 2}")
                    for c in range(NCH):
                        ii = te.matmul(pp, istk[:, :], nTP[u % 2][:, c, :],
                                       start=(c == 0), stop=(c == NCH - 1))
                    I(ii, "pe", key=f"dPp_{u}")

                def pr_(u):
                    ps = psA if u % 2 == 0 else psC
                    W(te, f"dVout_{u - 2}")
                    if u == 0:
                        W(te, "w_dsh")
                        W(te, f"cVW_{NCH - 1}")
                    for w in range(NCH):
                        for nh in range(2):
                            te.matmul(ps[:, nh * 512:(nh + 1) * 512],
                                      nTP[u % 2][:, w, :],
                                      wfT[:, w, nh * 512:(nh + 1) * 512],
                                      start=(w == 0), stop=False)
                    W(te, f"dVpo_{u}")
                    for nh in range(2):
                        ii = te.matmul(ps[:, nh * 512:(nh + 1) * 512],
                                       poolTP[u % 2][:, :],
                                       dshb[:, nh * 512:(nh + 1) * 512],
                                       start=False, stop=True)
                    I(ii, "pe", key=f"dPr_{u}")

                for p in range(1, T // 2):
                    t, t2 = 2 * p, 2 * p + 1
                    pn_(t - 2)
                    pn_(t2 - 2)
                    pp_(t - 2)
                    pp_(t2 - 2)
                    pr_(t - 2)
                    pr_(t2 - 2)
                pn_(T - 2)
                pn_(T - 1)
                pp_(T - 2)
                pp_(T - 1)
                pr_(T - 2)
                pr_(T - 1)

        drain_tail()
    return nc, ct


def build_full(debug=False, phases=5):
    _, ct1 = build_nc(debug=debug, phases=phases)
    nc, _ = build_nc(debug=debug, phases=phases, marks=ct1.marks)
    return nc


_DBG = [False, 5]


def kernel(**inputs):
    x = np.asarray(inputs["L0_attn_src"], dtype=np.float32)
    B = x.shape[0]
    w = {k: np.asarray(v, dtype=np.float32) for k, v in inputs.items()}

    ident = np.eye(P, dtype=np.float32)
    shared = {
        "Wq": w["L0_attn_Wq"], "Wk": w["L0_attn_Wk"], "Wv": w["L0_attn_Wv"],
        "Wo": w["L0_attn_Wo"], "W1": w["L0_attn_W1"], "W2": w["L0_attn_W2"],
        "Wf": w["L0_ffn_weight"], "disha": w["L0_ffn_disha"],
        "bq": w["L0_attn_bq"].reshape(1, D),
        "bk": w["L0_attn_bk"].reshape(1, D),
        "bv": w["L0_attn_bv"].reshape(1, D),
        "bo": w["L0_attn_bo"].reshape(1, D),
        "b1c": np.ascontiguousarray(
            w["L0_attn_b1"].reshape(NFCH, P).T),
        "b2": w["L0_attn_b2"].reshape(1, D),
        "g1r": np.ascontiguousarray(np.broadcast_to(w["L0_attn_g1"], (P, D))),
        "be1r": np.ascontiguousarray(np.broadcast_to(w["L0_attn_be1"], (P, D))),
        "g2r": np.ascontiguousarray(np.broadcast_to(w["L0_attn_g2"], (P, D))),
        "be2r": np.ascontiguousarray(np.broadcast_to(w["L0_attn_be2"], (P, D))),
        "identb": ident.astype(BF),
        "identf": ident,
        "ipadlo": np.concatenate([np.eye(DH), np.zeros((DH, DH))],
                                 axis=1).astype(BF),
        "ipadhi": np.concatenate([np.zeros((DH, DH)), np.eye(DH)],
                                 axis=1).astype(BF),
        "istack": np.concatenate([np.eye(DH), np.eye(DH)], axis=0).astype(BF),
        "onesr": np.ones((1, 512), dtype=np.float32).astype(BF),
        "onesc": np.ones((P, 1), dtype=np.float32).astype(BF),
        "epsz": np.full((1, H), EPS_Z, dtype=np.float32).astype(BF),
    }

    nc = build_full(debug=_DBG[0], phases=_DBG[1])
    in_maps = []
    for b in range(B):
        m = dict(shared)
        m["x"] = np.ascontiguousarray(x[b])
        in_maps.append(m)
    res = run_bass_kernel_spmd(nc, in_maps, core_ids=list(range(B)))
    out = np.stack([np.asarray(r["out"]) for r in res.results], axis=0)
    return out.astype(np.float32)
